# version 31
# baseline (speedup 1.0000x reference)
import numpy as np
import ml_dtypes
import concourse.bass as bass
import concourse.mybir as mybir
from concourse.bass_utils import run_bass_kernel_spmd

F32 = mybir.dt.float32
BF16 = mybir.dt.bfloat16
I32 = mybir.dt.int32
AF = mybir.ActivationFunctionType
ALU = mybir.AluOpType
AX = mybir.AxisListType

DM = 2048
SEQ = 2048
BATCH = 4
NCORES = 8
SLOTS = 1536
TC = 12
IFC = 16
ICN = 64
NQH = 32
HD = 64
NEG = -30000.0 * 0.125

KB = 512


def _sz(dt):
    return {F32: 4, BF16: 2, I32: 4}[dt]


class Sem:
    def __init__(self, stack, nc, name):
        self.h = stack.enter_context(nc.semaphore(name))
        self.n = 0

    def inc(self, ins, k=1):
        ins.then_inc(self.h, k)
        self.n += k
        return self.n


def build_program(dbg=False):
    from contextlib import ExitStack
    nc = bass.Bass("TRN2", target_bir_lowering=False)

    x_f = nc.dram_tensor("x_f", [DM, SLOTS], F32, kind="ExternalInput").ap()
    wq_pre = nc.dram_tensor("wq_pre", [IFC, 128, 3072], F32, kind="ExternalInput").ap()
    wo_pre = nc.dram_tensor("wo_pre", [4, IFC, 128, 4, 128], F32, kind="ExternalInput").ap()
    g_pre = nc.dram_tensor("g_pre", [ICN, 128, IFC, 128], I32, kind="ExternalInput").ap()
    u_pre = nc.dram_tensor("u_pre", [ICN, 128, IFC, 128], I32, kind="ExternalInput").ap()
    d_pre = nc.dram_tensor("d_pre", [ICN, 128, IFC, 128], I32, kind="ExternalInput").ap()
    gs_r = nc.dram_tensor("gs_r", [128, ICN], F32, kind="ExternalInput").ap()
    us_r = nc.dram_tensor("us_r", [128, ICN], F32, kind="ExternalInput").ap()
    ds_r = nc.dram_tensor("ds_r", [128, IFC], F32, kind="ExternalInput").ap()
    cos_in = nc.dram_tensor("cos_in", [128, TC, HD], F32, kind="ExternalInput").ap()
    sin_in = nc.dram_tensor("sin_in", [128, TC, HD], F32, kind="ExternalInput").ap()
    qnw_in = nc.dram_tensor("qnw_in", [1, HD], F32, kind="ExternalInput").ap()
    knw_in = nc.dram_tensor("knw_in", [1, HD], F32, kind="ExternalInput").ap()
    expb_in = nc.dram_tensor("expb_in", [128, 4], F32, kind="ExternalInput").ap()
    y = nc.dram_tensor("y", [DM, 1024], F32, kind="ExternalOutput").ap()
    attn_scr = nc.dram_tensor("attn_scr", [IFC, 128, 1024], BF16, kind="Internal").ap()
    wd_scr = nc.dram_tensor("wd_scr", [ICN, 128, IFC, 128], BF16, kind="Internal").ap()
    dbg_outs = {}
    if dbg:
        for nm, shape, dt in [
            ("dbg_xn", [128, IFC * SLOTS], BF16),
            ("dbg_qf", [128, 16 * 1024], BF16),
            ("dbg_kf", [128, 8 * SLOTS], BF16),
            ("dbg_va", [128, TC * 520], BF16),
            ("dbg_atm", [128, 8 * 2048], BF16),
            ("dbg_att", [128, 16 * 1024], BF16),
            ("dbg_sx", [2, 1024], BF16),
            ("dbg_sh", [2, 1024], BF16),
            ("dbg_h", [128, 2 * 1024], BF16),
        ]:
            dbg_outs[nm] = nc.dram_tensor(nm, shape, dt, kind="ExternalOutput").ap()

    stack = ExitStack()

    ARENA_KB = 200
    AR = stack.enter_context(nc.sbuf_tensor("AR", [128, ARENA_KB * KB], BF16))

    def av(off_kb, shape, dt=BF16, p0=0):
        nfree = 1
        for s in shape[1:]:
            nfree *= s
        nb = nfree * _sz(dt)
        e0 = int(off_kb * KB)
        assert e0 * 2 == int(off_kb * 1024), off_kb
        ap = AR[p0:p0 + shape[0], e0: e0 + nb // 2]
        if dt != BF16:
            ap = ap.bitcast(dt)
        if len(shape) == 3:
            ap = ap.rearrange("p (a b) -> p a b", a=shape[1], b=shape[2])
        elif len(shape) == 4:
            ap = ap.rearrange("p (a b c) -> p a b c", a=shape[1], b=shape[2], c=shape[3])
        assert list(ap.shape) == list(shape), (ap.shape, shape)
        return ap

    sb = lambda name, shape, dt: stack.enter_context(nc.sbuf_tensor(name, shape, dt))
    ident_f = sb("ident_f", [128, 128], F32)
    ident_b = sb("ident_b", [128, 128], BF16)
    m0 = sb("m0", [128, 128], BF16)
    m4 = sb("m4", [128, 128], BF16)
    m0aT = sb("m0aT", [128, 128], BF16)
    m4aT = sb("m4aT", [128, 128], BF16)
    ones128 = sb("ones128", [128, 1], BF16)
    onesrow = sb("onesrow", [1, 128], BF16)
    negrow = sb("negrow", [1, 128], BF16)
    cm128 = sb("cm128", [2, 128], BF16)
    eps_ln = sb("eps_ln", [1, 1], F32)
    eps12 = sb("eps12", [12, 1], F32)
    eps_rms = sb("eps_rms", [128, 1], F32)
    expb = sb("expb", [128, 4], F32)
    gs_s = sb("gs_s", [128, ICN], F32)
    us_s = sb("us_s", [128, ICN], F32)
    ds_s = sb("ds_s", [128, IFC], F32)
    qnw = sb("qnw", [1, HD], F32)
    knw = sb("knw", [1, HD], F32)
    sxTb = sb("sxTb", [128, 1024], BF16)
    rcol = sb("rcol", [128, 1], F32)
    rcol4 = sb("rcol4", [128, 4], F32)
    ssq = sb("ssq", [128, 8], F32)
    srt = sb("srt", [128, 8], F32)
    rstd = sb("rstd", [128, 8], F32)

    xn = av(0, [128, IFC, SLOTS], BF16)
    xdb = [av(96, [128, SLOTS], F32), av(102, [128, SLOTS], F32)]
    sc12 = av(96, [12, 2, 128], F32)
    sct_a = av(97, [12, 128], F32)
    sct_b = av(97.5, [12, 128], F32)
    scb12 = av(98, [12, 2, 128], BF16)
    xsq = [av(108, [128, SLOTS], BF16), av(111, [128, SLOTS], BF16)]
    tmp_b = av(117, [128, SLOTS], BF16)
    rstdB = av(120, [128, SLOTS], BF16)
    mrsB = av(123, [128, SLOTS], BF16)
    cos_t = av(132, [128, TC, HD], F32)
    sin_t = av(135, [128, TC, HD], F32)
    r_a = av(138, [1, SLOTS], F32)
    r_b = av(144, [1, SLOTS], F32)
    r_c = av(150, [1, SLOTS], F32)
    r_d = av(156, [1, SLOTS], F32)
    rb_rstd = av(162, [1, SLOTS], BF16)
    rb_mrs = av(165, [1, SLOTS], BF16)
    cwq = av(176.5, [128, TC, HD], F32)
    swq = av(179.5, [128, TC, HD], F32)
    cwk = av(182.5, [128, TC, HD], F32)
    swk = av(185.5, [128, TC, HD], F32)
    wtmp = [av(188.5, [128, HD], F32), av(188.75, [128, HD], F32),
            av(189, [128, HD], F32), av(189.25, [128, HD], F32)]
    qnb = av(189.5, [1, HD], BF16)
    knb = av(189.75, [1, HD], BF16)
    sxlo = av(190, [1, 1024], BF16)
    wq_b = av(48, [128, IFC, 1536], BF16)
    q_f = av(96, [128, 16, 1024], BF16)
    k2v = av(128, [128, 4, 2, SLOTS], BF16)
    k_f8 = av(128, [128, 8, SLOTS], BF16)
    v_a = av(152, [128, TC, 520], BF16)
    b_q2 = av(164.5, [128, 512], F32)
    b_qs = av(166.5, [128, 512], F32)
    b_t1 = av(168.5, [128, 512], F32)
    b_t2 = av(170.5, [128, 512], F32)
    b_t3 = av(172.5, [128, 512], F32)
    b_qf2 = [av(174.5, [128, 512], F32), av(190.5, [128, 512], F32),
             av(188.5, [128, 512], F32), av(192.5, [128, 512], F32)]
    atm = av(48, [128, 8, 2048], BF16)
    pbufq = [[av(80, [128, 640], BF16), av(81.25, [128, 640], BF16)],
             [av(82.5, [128, 640], BF16), av(83.75, [128, 640], BF16)]]
    att = av(96, [128, 16, 1024], BF16)
    woh = av(180, [128, IFC, 4, 128], BF16)
    woh2 = av(48, [128, IFC, 4, 128], BF16)
    dstg = [av(64, [128, 4, 512], BF16), av(68, [128, 4, 512], BF16)]
    h_t = av(48, [128, ICN, 1024], BF16)
    gt = [av(176, [128, IFC, 128], BF16), av(180, [128, IFC, 128], BF16)]
    ut = [av(184, [128, IFC, 128], BF16), av(188, [128, IFC, 128], BF16)]
    sg = [av(192, [128, 1024], BF16), av(194, [128, 1024], BF16)]
    shlo = av(192, [1, 1024], BF16)
    sx2 = av(196, [2, 1024], BF16)
    sh2 = av(198, [2, 1024], BF16)
    dtl = [av(176, [128, 8, 4, 128], BF16), av(184, [128, 8, 4, 128], BF16),
           av(40, [128, 8, 4, 128], BF16)]
    xres = [av(0, [128, 4, 512], F32), av(8, [128, 4, 512], F32)]
    arel = [av(16, [128, 4, 512], BF16), av(20, [128, 4, 512], BF16)]
    outst = [av(24, [128, 4, 512], F32), av(32, [128, 4, 512], F32)]

    dX2 = [Sem(stack, nc, "dX0"), Sem(stack, nc, "dX1")]
    dTAB = Sem(stack, nc, "dTAB")
    dWQ4 = [Sem(stack, nc, "dWQa"), Sem(stack, nc, "dWQb"),
            Sem(stack, nc, "dWQc"), Sem(stack, nc, "dWQd")]
    dWO = Sem(stack, nc, "dWO")
    dG2 = [Sem(stack, nc, "dG0"), Sem(stack, nc, "dG1")]
    dU2 = [Sem(stack, nc, "dU0"), Sem(stack, nc, "dU1")]
    dD2 = [Sem(stack, nc, "dD0"), Sem(stack, nc, "dD1"), Sem(stack, nc, "dD2")]
    dXR2 = [Sem(stack, nc, "dXR0"), Sem(stack, nc, "dXR1")]
    dAS2 = [Sem(stack, nc, "dAS0"), Sem(stack, nc, "dAS1")]
    dAR2 = [Sem(stack, nc, "dAR0"), Sem(stack, nc, "dAR1")]
    dOUT2 = [Sem(stack, nc, "dOUT0"), Sem(stack, nc, "dOUT1")]
    dWC = Sem(stack, nc, "dWC")
    dROW = Sem(stack, nc, "dROW")
    dSC = Sem(stack, nc, "dSC")
    dDBG = Sem(stack, nc, "dDBG")
    tT = Sem(stack, nc, "tT"); tV = Sem(stack, nc, "tV")
    tS = Sem(stack, nc, "tS"); tG = Sem(stack, nc, "tG")

    T, V, S, G, SY = nc.tensor, nc.vector, nc.scalar, nc.gpsimd, nc.sync

    def vdo(make):
        V.wait_ge(tV.h, tV.n)
        return tV.inc(make())

    def sdo(make):
        S.wait_ge(tS.h, tS.n)
        return tS.inc(make())

    def gdo(make):
        G.wait_ge(tG.h, tG.n)
        return tG.inc(make())

    def barrier():
        counts = [(tT, tT.n), (tV, tV.n), (tS, tS.n), (tG, tG.n)]
        for e in (T, V, S, G, SY):
            for sm, n in counts:
                if n:
                    e.wait_ge(sm.h, n)

    x_r = x_f.rearrange("(c p) s -> c p s", p=128)
    SY.dma_start(xdb[0], x_r[0]).then_inc(dX2[0].h, 16)
    dX2[0].n += 16
    S.dma_start(xdb[1], x_r[1]).then_inc(dX2[1].h, 16)
    dX2[1].n += 16

    for dst, src in ((cos_t, cos_in), (sin_t, sin_in), (qnw[:], qnw_in),
                     (knw[:], knw_in), (gs_s[:], gs_r), (us_s[:], us_r),
                     (ds_s[:], ds_r), (expb[:], expb_in)):
        S.dma_start(dst, src).then_inc(dTAB.h, 16)
        dTAB.n += 16

    gdo(lambda: G.memset(ones128[:], 1.0))
    gdo(lambda: G.memset(onesrow[:], 1.0))
    gdo(lambda: G.memset(negrow[:], -1.0))
    gdo(lambda: G.memset(cm128[:], -128.0))
    gdo(lambda: G.memset(eps_ln[:], 1e-5))
    gdo(lambda: G.memset(eps12[:], 1e-5))
    gdo(lambda: G.memset(eps_rms[:], 1e-6))
    g_ones = tG.n
    gdo(lambda: G.memset(ident_f[:], 0.0))
    gdo(lambda: G.affine_select(out=ident_f[:], in_=ident_f[:], compare_op=ALU.not_equal,
                                fill=1.0, base=0, pattern=[[-1, 128]], channel_multiplier=1))
    gdo(lambda: G.memset(ident_b[:], 0.0))
    gdo(lambda: G.affine_select(out=ident_b[:], in_=ident_b[:], compare_op=ALU.not_equal,
                                fill=1.0, base=0, pattern=[[-1, 128]], channel_multiplier=1))
    gdo(lambda: G.memset(m0[:], 1.0))
    gdo(lambda: G.affine_select(out=m0[:], in_=m0[:], compare_op=ALU.is_gt,
                                fill=0.0, base=0, pattern=[[-1, 128]], channel_multiplier=1))
    gdo(lambda: G.memset(m4[:], 1.0))
    gdo(lambda: G.affine_select(out=m4[:], in_=m4[:], compare_op=ALU.is_ge,
                                fill=0.0, base=0, pattern=[[1, 128]], channel_multiplier=-1))
    gdo(lambda: G.memset(m0aT[:], 0.0))
    gdo(lambda: G.affine_select(out=m0aT[:], in_=m0aT[:], compare_op=ALU.is_gt,
                                fill=-30000.0, base=0, pattern=[[1, 128]], channel_multiplier=-1))
    gdo(lambda: G.memset(m4aT[:], 0.0))
    gdo(lambda: G.affine_select(out=m4aT[:], in_=m4aT[:], compare_op=ALU.is_ge,
                                fill=-30000.0, base=0, pattern=[[-1, 128]], channel_multiplier=1))
    G.wait_ge(dX2[0].h, 16 * 5)
    for j in range(4):
        G.dma_start(
            wq_b[:, 4 * j:4 * j + 4, :],
            wq_pre[4 * j:4 * j + 4, :, 0:1536].transpose([1, 0, 2]),
        ).then_inc(dWQ4[j].h, 16)
        dWQ4[j].n += 16
    G.wait_ge(dTAB.h, dTAB.n)
    gdo(lambda: G.tensor_copy(qnb, qnw[:]))
    g_rows = gdo(lambda: G.tensor_copy(knb, knw[:]))

    S.preload_act_table(AF.Sqrt)
    S.wait_ge(tG.h, g_ones)
    T.wait_ge(tG.h, g_ones)

    with (
        nc.psum_tensor("pa0", [128, 512], F32) as pa0,
        nc.psum_tensor("pa1", [128, 512], F32) as pa1,
        nc.psum_tensor("pa2", [128, 512], F32) as pa2,
        nc.psum_tensor("pa3", [128, 512], F32) as pa3,
        nc.psum_tensor("pa4", [128, 512], F32) as pa4,
        nc.psum_tensor("pa5", [128, 512], F32) as pa5,
        nc.psum_tensor("pa6", [128, 512], F32) as pa6,
        nc.psum_tensor("pa7", [128, 512], F32) as pa7,
    ):
        psx = [pa0, pa1, pa2]
        psq = [pa3, pa4, pa5]
        psb = [pa6, pa7]
        sq_done, cp_done, mm_done = {}, {}, {}
        t_ropemm = 0
        for ci in range(IFC):
            par = ci % 2
            eng = SY if par == 0 else S
            if ci >= 2:
                eng.wait_ge(tS.h, sq_done[ci - 2])
                eng.wait_ge(tV.h, cp_done[ci - 2])
                eng.dma_start(xdb[par], x_r[ci]).then_inc(dX2[par].h, 16)
                dX2[par].n += 16
            S.wait_ge(dX2[par].h, 16 * (ci // 2 + 1))
            if ci >= 2:
                S.wait_ge(tT.h, mm_done[ci - 2])
            sq_done[ci] = sdo(lambda par=par: S.activation(xsq[par], xdb[par], AF.Square))
            V.wait_ge(dX2[par].h, 16 * (ci // 2 + 1))
            cp_done[ci] = vdo(lambda ci=ci, par=par: V.tensor_copy(xn[:, ci, :], xdb[par]))
            T.wait_ge(tV.h, cp_done[ci])
            T.wait_ge(tS.h, sq_done[ci])
            last = None
            for sl in range(3):
                T.matmul(psx[sl][0:1, :], ones128[:], xn[:, ci, 512 * sl:512 * sl + 512],
                         start=(ci == 0), stop=(ci == IFC - 1))
                last = T.matmul(psq[sl][0:1, :], ones128[:],
                                xsq[par][:, 512 * sl:512 * sl + 512],
                                start=(ci == 0), stop=(ci == IFC - 1))
            mm_done[ci] = tT.inc(last)
            if ci == 6:
                T.wait_ge(tG.h, g_rows)
                T.matmul(psb[0][:, 0:HD], onesrow[:], qnb, start=True, stop=True)
                t_ropemm = tT.inc(T.matmul(psb[1][:, 0:HD], onesrow[:], knb,
                                           start=True, stop=True))
            if ci == 7:
                S.wait_ge(tT.h, t_ropemm)
                sdo(lambda: S.copy(wtmp[0], psb[0][:, 0:HD]))
                s_wtmp = sdo(lambda: S.copy(wtmp[1], psb[1][:, 0:HD]))
        G.wait_ge(tS.h, s_wtmp)
        gdo(lambda: G.tensor_copy(wtmp[2][:, 0:32], wtmp[0][:, 32:64]))
        gdo(lambda: G.tensor_copy(wtmp[2][:, 32:64], wtmp[0][:, 0:32]))
        gdo(lambda: G.tensor_copy(wtmp[3][:, 0:32], wtmp[1][:, 32:64]))
        gdo(lambda: G.tensor_copy(wtmp[3][:, 32:64], wtmp[1][:, 0:32]))
        G.wait_ge(dTAB.h, dTAB.n)
        gdo(lambda: G.tensor_mul(cwq, cos_t, wtmp[0].unsqueeze(1).broadcast_to([128, TC, HD])))
        gdo(lambda: G.tensor_mul(swq, sin_t, wtmp[2].unsqueeze(1).broadcast_to([128, TC, HD])))
        gdo(lambda: G.tensor_mul(cwk, cos_t, wtmp[1].unsqueeze(1).broadcast_to([128, TC, HD])))
        g_rope = gdo(lambda: G.tensor_mul(swk, sin_t,
                                          wtmp[3].unsqueeze(1).broadcast_to([128, TC, HD])))
        V.wait_ge(tT.h, mm_done[IFC - 1])
        v1 = 0
        for sl in range(3):
            vdo(lambda sl=sl: V.tensor_scalar_mul(r_a[:, 512 * sl:512 * sl + 512],
                                                  psx[sl][0:1, :], 1.0 / DM))
            v1 = vdo(lambda sl=sl: V.tensor_scalar_mul(r_b[:, 512 * sl:512 * sl + 512],
                                                       psq[sl][0:1, :], 1.0 / DM))
        SY.wait_ge(tV.h, v1)
        SY.wait_ge(tS.h, sq_done[IFC - 1])
        SY.wait_ge(dX2[0].h, dX2[0].n)
        mu12 = sc12[:, 0, :]
        m212 = sc12[:, 1, :]
        SY.dma_start(mu12, av(138, [1, 12, 128], F32)).then_inc(dSC.h, 16)
        SY.dma_start(m212, av(144, [1, 12, 128], F32)).then_inc(dSC.h, 16)
        dSC.n += 32
        V.wait_ge(dSC.h, dSC.n)
        vdo(lambda: V.tensor_mul(sct_a, mu12, mu12))
        v_var = vdo(lambda: V.scalar_tensor_tensor(sct_b, m212, 1.0, sct_a,
                                                   ALU.mult, ALU.subtract))
        S.wait_ge(tV.h, v_var)
        s2 = sdo(lambda: S.activation(sct_a, sct_b, AF.Sqrt, bias=eps12[:, 0:1]))
        V.wait_ge(tS.h, s2)
        vdo(lambda: V.reciprocal(sct_b, sct_a))
        vdo(lambda: V.tensor_mul(sct_a, mu12, sct_b))
        vdo(lambda: V.tensor_copy(scb12[:, 0, :], sct_b))
        v3 = vdo(lambda: V.tensor_copy(scb12[:, 1, :], sct_a))
        SY.wait_ge(tV.h, v3)
        SY.wait_ge(dSC.h, dSC.n)
        SY.dma_start(av(162, [1, 12, 128], BF16), scb12[:, 0, :]).then_inc(dSC.h, 16)
        SY.dma_start(av(165, [1, 12, 128], BF16), scb12[:, 1, :]).then_inc(dSC.h, 16)
        dSC.n += 32
        T.wait_ge(dSC.h, dSC.n)
        T.wait_ge(tS.h, s_wtmp)
        T.wait_ge(tV.h, v3)
        bk_r = [psb[0], psx[2], psq[1]]
        bk_m = [psb[1], psq[0], psq[2]]
        t_bc = 0
        for sl in range(3):
            T.matmul(bk_r[sl][:, :], onesrow[:],
                     rb_rstd[:, 512 * sl:512 * sl + 512], start=True, stop=True)
            t_bc = tT.inc(T.matmul(bk_m[sl][:, :], negrow[:],
                                   rb_mrs[:, 512 * sl:512 * sl + 512],
                                   start=True, stop=True))
        V.wait_ge(tT.h, t_bc)
        S.wait_ge(tT.h, t_bc)
        for sl in range(3):
            vdo(lambda sl=sl: V.tensor_copy(rstdB[:, 512 * sl:512 * sl + 512],
                                            bk_r[sl][:, :]))
            sdo(lambda sl=sl: S.copy(mrsB[:, 512 * sl:512 * sl + 512], bk_m[sl][:, :]))
        s_bc = tS.n
        V.wait_ge(tS.h, s_bc)
        last = None
        for ci in range(IFC):
            vdo(lambda ci=ci: V.tensor_mul(tmp_b, xn[:, ci, :], rstdB))
            xn_last = vdo(lambda ci=ci: V.tensor_add(xn[:, ci, :], tmp_b, mrsB))
            T.wait_ge(tV.h, xn_last)
            for sl in range(2):
                last = T.matmul(psx[sl][0:1, :], ones128[:],
                                xn[:, ci, 512 + 512 * sl:1024 + 512 * sl],
                                start=(ci == 0), stop=(ci == IFC - 1))
        t_sx = tT.inc(last)
        V.wait_ge(tT.h, t_sx)
        vdo(lambda: V.tensor_copy(sx2[0:1, 0:512], psx[0][0:1, :]))
        vdo(lambda: V.tensor_copy(sx2[0:1, 512:1024], psx[1][0:1, :]))
        vdo(lambda: V.tensor_sub(sxlo[:, 0:512], psx[0][0:1, :], sx2[0:1, 0:512]))
        v_sxl = vdo(lambda: V.tensor_sub(sxlo[:, 512:1024], psx[1][0:1, :], sx2[0:1, 512:1024]))
        G.wait_ge(tV.h, v_sxl)
        G.dma_start(sx2[1:2, :], sxlo[:]).then_inc(dROW.h, 16)
        dROW.n += 16
        sx_row_cnt = dROW.n

    barrier()
    gdo(lambda: G.memset(v_a.rearrange("p t (g e) -> p t g e", e=65)[:, :, :, 64:65], 1.0))
    if dbg:
        SY.wait_ge(dROW.h, sx_row_cnt)
        SY.dma_start(dbg_outs["dbg_xn"], xn.rearrange("p a b -> p (a b)")).then_inc(dDBG.h, 16)
        SY.dma_start(dbg_outs["dbg_sx"], sx2[:]).then_inc(dDBG.h, 16)
        dDBG.n += 32

    with (
        nc.psum_tensor("pb0", [128, 512], F32) as pb0,
        nc.psum_tensor("pb1", [128, 512], F32) as pb1,
        nc.psum_tensor("pb2", [128, 512], F32) as pb2,
        nc.psum_tensor("pb3", [128, 512], F32) as pb3,
        nc.psum_tensor("pb4", [128, 512], F32) as pb4,
        nc.psum_tensor("pb5", [128, 512], F32) as pb5,
        nc.psum_tensor("pT0", [128, 512], F32) as pT0,
        nc.psum_tensor("pT1", [128, 512], F32) as pT1,
    ):
        psB = [[pb0, pb1], [pb2, pb3], [pb4, pb5]]
        psT = [pT0, pT1]
        V.wait_ge(dROW.h, sx_row_cnt)
        ev_s, ev_v, mm_done = {}, {}, {}
        ptr = [0]
        psT_copy = {0: 0, 1: 0}
        pend = []
        qf_slot = [0]
        qf_rd = {}

        def emit_tr():
            while pend:
                s, t, tl, v_qf, slot = pend.pop(0)
                is_q = s <= 3
                qfbuf = b_qf2[slot]
                tp = ptr[0] % 2
                ptr[0] += 1
                T.wait_ge(tV.h, v_qf)
                T.wait_ge(tV.h, psT_copy[tp])
                lastt = None
                for u in range(4):
                    lastt = T.transpose(psT[tp][:, 128 * u:128 * u + 128],
                                        qfbuf[:, 128 * u:128 * u + 128], ident_f[:])
                t_tr = tT.inc(lastt)
                qf_rd[slot] = t_tr
                V.wait_ge(tT.h, t_tr)
                p3 = psT[tp][:, :].rearrange("p (u j) -> p u j", u=4)
                if is_q:
                    psT_copy[tp] = vdo(lambda s=s, tl=tl, p3=p3: V.tensor_copy(
                        q_f[:, 4 * s:4 * s + 4, 128 * tl:128 * tl + 128], p3))
                else:
                    vdo(lambda t=t, p3=p3: V.tensor_copy(
                        k2v[0:64, :, 0, 128 * t:128 * t + 128], p3[0:64, :, :]))
                    psT_copy[tp] = vdo(lambda t=t, p3=p3: V.tensor_copy(
                        k2v[64:128, :, 1, 128 * t:128 * t + 128], p3[64:128, :, :]))

        def emit_mm(grp, t, par, slices):
            for e_si, s in slices:
                if (e_si, par) in ev_s:
                    T.wait_ge(tS.h, ev_s[(e_si, par)])
                    T.wait_ge(tV.h, ev_v[(e_si, par)])
            last = None
            for ci in range(IFC):
                if ci % 4 == 0:
                    T.wait_ge(dWQ4[ci // 4].h, 16 * (grp + 1))
                for e_si, s in slices:
                    last = T.matmul(psB[e_si][par][:, :],
                                    xn[:, ci, 128 * t:128 * t + 128],
                                    wq_b[:, ci, 512 * (s - 3 * grp):512 * (s - 3 * grp) + 512],
                                    start=(ci == 0), stop=(ci == IFC - 1))
            mm_done[t] = tT.inc(last)

        def emit_evac(grp, t, par, slices):
            tl = t - 4
            for e_si, s in slices:
                ps = psB[e_si][par]
                if s <= 4:
                    is_q = s <= 3
                    cw, sw = (cwq, swq) if is_q else (cwk, swk)
                    S.wait_ge(tT.h, mm_done[t])
                    S.wait_ge(tV.h, ev_v.get("red", 0))
                    s_sq = sdo(lambda ps=ps: S.activation(b_q2, ps[:, :], AF.Square))
                    V.wait_ge(tS.h, s_sq)
                    V.wait_ge(tS.h, ev_s.get("srt", 0))
                    v_red = vdo(lambda: V.tensor_reduce(
                        ssq[:, 0:8].rearrange("p (g o) -> p g o", o=1),
                        b_q2.rearrange("p (g j) -> p g j", g=8), AX.X, ALU.add))
                    ev_v["red"] = v_red
                    S.wait_ge(tV.h, v_red)
                    S.wait_ge(tV.h, ev_v.get("rcp", 0))
                    s_srt = sdo(lambda: S.activation(srt[:, 0:8], ssq[:, 0:8], AF.Sqrt,
                                                     bias=eps_rms[:, 0:1], scale=1.0 / HD))
                    ev_s["srt"] = s_srt
                    V.wait_ge(tS.h, s_srt)
                    ev_v["rcp"] = vdo(lambda: V.reciprocal(rstd[:, 0:8], srt[:, 0:8]))
                    ps4d = ps[:, :].rearrange("p (g v j) -> p g v j", g=8, v=2)
                    cwb = cw[:, t, :].unsqueeze(1).broadcast_to([128, 8, HD])
                    swb0 = sw[:, t, 0:32].unsqueeze(1).broadcast_to([128, 8, 32])
                    swb1 = sw[:, t, 32:64].unsqueeze(1).broadcast_to([128, 8, 32])
                    t13 = b_t1.rearrange("p (g j) -> p g j", g=8)
                    t23 = b_t2.rearrange("p (g j) -> p g j", g=8)
                    t24 = b_t2.rearrange("p (g v j) -> p g v j", g=8, v=2)
                    t33 = b_t3.rearrange("p (g j) -> p g j", g=8)
                    V.wait_ge(tT.h, mm_done[t])
                    ev_pb = vdo(lambda ps=ps, t13=t13, cwb=cwb: V.tensor_mul(
                        t13, ps[:, :].rearrange("p (g j) -> p g j", g=8), cwb))
                    vdo(lambda t24=t24, ps4d=ps4d, swb0=swb0: V.tensor_mul(
                        t24[:, :, 0, :], ps4d[:, :, 1, :], swb0))
                    v_t2 = vdo(lambda t24=t24, ps4d=ps4d, swb1=swb1: V.tensor_mul(
                        t24[:, :, 1, :], ps4d[:, :, 0, :], swb1))
                    vdo(lambda t33=t33, t13=t13, t23=t23: V.tensor_add(t33, t13, t23))
                    rsb = rstd[:, 0:8].unsqueeze(2).broadcast_to([128, 8, HD])
                    qfbuf = b_qf2[qf_slot[0]]
                    qf3 = qfbuf.rearrange("p (g j) -> p g j", g=8)
                    V.wait_ge(tT.h, qf_rd.get(qf_slot[0], 0))
                    v_qf = vdo(lambda qf3=qf3, t33=t33, rsb=rsb: V.tensor_mul(qf3, t33, rsb))
                    ev_s[(e_si, par)] = s_sq
                    ev_v[(e_si, par)] = v_t2
                    pend.append((s, t, tl, v_qf, qf_slot[0]))
                    qf_slot[0] = (qf_slot[0] + 1) % 4
                else:
                    V.wait_ge(tT.h, mm_done[t])
                    vv = vdo(lambda t=t, ps=ps: V.tensor_copy(
                        v_a[:, t, :].rearrange("p (g e) -> p g e", e=65)[:, :, 0:64],
                        ps[:, :].rearrange("p (g j) -> p g j", g=8)))
                    ev_s[(e_si, par)] = tS.n
                    ev_v[(e_si, par)] = vv

        for ti, t in enumerate(range(4, 12)):
            emit_mm(0, t, ti % 2, [(0, 0), (1, 1), (2, 2)])
            emit_tr()
            emit_evac(0, t, ti % 2, [(0, 0), (1, 1), (2, 2)])
        emit_tr()
        G.wait_ge(tT.h, mm_done[11])
        for j in range(4):
            G.wait_ge(dWQ4[j].h, dWQ4[j].n)
            G.dma_start(
                wq_b[:, 4 * j:4 * j + 4, :],
                wq_pre[4 * j:4 * j + 4, :, 1536:3072].transpose([1, 0, 2]),
            ).then_inc(dWQ4[j].h, 16)
            dWQ4[j].n += 16
        for ti, t in enumerate(range(0, 12)):
            sl = [(0, 3), (1, 4), (2, 5)] if t >= 4 else [(1, 4), (2, 5)]
            emit_mm(1, t, ti % 2, sl)
            emit_tr()
            emit_evac(1, t, ti % 2, sl)
        emit_tr()

    barrier()
    G.wait_ge(dROW.h, dROW.n)
    G.dma_start(k2v[64:128, :, 0, :], k2v[0:64, :, 0, :]).then_inc(dROW.h, 16)
    G.dma_start(k2v[0:64, :, 1, :], k2v[64:128, :, 1, :]).then_inc(dROW.h, 16)
    dROW.n += 32
    krep_cnt = dROW.n
    if dbg:
        SY.wait_ge(dROW.h, krep_cnt)
        SY.dma_start(dbg_outs["dbg_qf"], q_f.rearrange("p a b -> p (a b)")).then_inc(dDBG.h, 16)
        SY.dma_start(dbg_outs["dbg_kf"], k_f8.rearrange("p a b -> p (a b)")).then_inc(dDBG.h, 16)
        SY.dma_start(dbg_outs["dbg_va"], v_a.rearrange("p a b -> p (a b)")).then_inc(dDBG.h, 16)
        dDBG.n += 48

    S.preload_act_table(AF.Exp)
    G.wait_ge(tT.h, tT.n)
    G.wait_ge(dWO.h, dWO.n)
    G.dma_start(woh[:, :, :, :], wo_pre[0].transpose([1, 0, 2, 3])).then_inc(dWO.h, 16)
    dWO.n += 16
    for j in range(8):
        G.dma_start(wd_scr[8 * j:8 * j + 8], d_pre[8 * j:8 * j + 8]).then_inc(dWC.h, 16)
        dWC.n += 16

    with (
        nc.psum_tensor("scmA", [128, 1024], F32) as scmA,
        nc.psum_tensor("sct0", [128, 512], F32) as sct0,
        nc.psum_tensor("scmB", [128, 1024], F32) as scmB,
        nc.psum_tensor("sct1", [128, 512], F32) as sct1,
        nc.psum_tensor("pvp0", [128, 512], F32) as pvp0,
        nc.psum_tensor("pvp1", [128, 512], F32) as pvp1,
    ):
        scm = [[scmA[:, 0:512], scmA[:, 512:1024]], [scmB[:, 0:512], scmB[:, 512:1024]]]
        sct = [sct0, sct1]
        pvp = [pvp0, pvp1]
        T.wait_ge(dROW.h, krep_cnt)

        exp_done = {0: 0, 1: 0}
        pv_done = {0: 0, 1: 0}
        ev_done = {0: 0, 1: 0}
        smm = {0: 0, 1: 0}

        def mk_scores(par, hA, qb):
            ops = []
            order = [(0, 0), (1, 0), (0, 1), (1, 1), (0, 2), (1, 2),
                     (0, 3), (1, 3), (0, 4), (1, 4)]
            order = sorted(order)
            for lane, o in order:
                hh = hA + lane
                kv = hh // 4
                qpair, qrow = hh // 2, 64 * (hh % 2)
                kb = qb + o
                lhs = k_f8[qrow:qrow + 64, kv, 128 * kb:128 * kb + 128]
                dst = (scm[par][lane][:, 128 * o:128 * o + 128] if o < 4
                       else sct[par][:, 128 * lane:128 * lane + 128])
                rhs = q_f[qrow:qrow + 64, qpair, 128 * qb:128 * qb + 128]
                start = (o == 0) if o < 4 else (lane == 0)
                stop = o in (1, 2, 3)
                ops.append(lambda dst=dst, lhs=lhs, rhs=rhs, start=start, stop=stop: T.matmul(
                    dst, lhs, rhs, start=start, stop=stop, skip_group_check=True))
            return ops

        def mk_pv(par, hA, qb):
            ops = []
            for lane in range(2):
                kv = (hA + lane) // 4
                pb = pbufq[par][lane]
                for o in range(5):
                    kb = qb + o
                    ops.append(lambda par=par, lane=lane, pb=pb, o=o, kb=kb, kv=kv:
                               T.matmul(pvp[par][:, 65 * lane:65 * lane + 65],
                                        pb[:, 128 * o:128 * o + 128],
                                        v_a[:, kb, 65 * kv:65 * kv + 65],
                                        start=(o == 0), stop=(o == 4),
                                        skip_group_check=True))
            return ops

        def mk_dummy():
            return [lambda: T.matmul(pvp[1][:, 0:64], ident_b[:], ident_b[:, 0:64],
                                     start=True, stop=True, skip_group_check=True)
                    for _ in range(10)]

        def c_exp(par, qb):
            S.wait_ge(tT.h, smm[par])
            S.wait_ge(tT.h, pv_done[par])
            pb2 = av(80 + 2.5 * par, [128, 2, 640], BF16)
            smv = [scmA, scmB][par][:, :].rearrange("p (l c) -> p l c", l=2)
            e = None
            if qb < 4:
                c = 128 * (4 - qb)
                e = sdo(lambda c=c: S.activation(
                    pb2[:, :, 0:c], smv[:, :, 0:c], AF.Exp,
                    bias=expb[:, qb:qb + 1], scale=0.125))
                if c < 512:
                    e = sdo(lambda c=c: S.activation(
                        pb2[:, :, c:512], smv[:, :, c:512], AF.Exp,
                        bias=0.0, scale=0.125))
            else:
                e = sdo(lambda: S.activation(
                    pb2[:, :, 0:512], smv, AF.Exp, bias=0.0, scale=0.125))
            e = sdo(lambda: S.activation(
                pb2[:, :, 512:640],
                sct[par][:, 0:256].rearrange("p (l c) -> p l c", l=2),
                AF.Exp, bias=0.0, scale=0.125))
            exp_done[par] = e
            return e, None

        def c_evac(par, hA, qb, pvm):
            V.wait_ge(tT.h, pvm)
            ev = None
            for lane in range(2):
                hh = hA + lane
                vdo(lambda par=par, lane=lane: V.reciprocal(
                    rcol[:, 0:1], pvp[par][:, 65 * lane + 64:65 * lane + 65]))
                ev = vdo(lambda par=par, lane=lane, qb=qb, hh=hh: V.tensor_scalar_mul(
                    atm[:, qb, 64 * hh:64 * hh + 64],
                    pvp[par][:, 65 * lane:65 * lane + 64], rcol[:, 0:1]))
            ev_done[par] = ev

        def c_scores(par, hA, qb):
            T.wait_ge(tS.h, exp_done[par])
            s_ops = mk_scores(par, hA, qb)
            last = None
            for lane in range(2):
                for i in range(5 * lane, 5 * lane + 5):
                    s_ops[i]()
                T.matmul(scm[par][lane][:, 0:128], m0aT[:], ident_b[:],
                         start=False, stop=True, skip_group_check=True)
                last = T.matmul(sct[par][:, 128 * lane:128 * lane + 128],
                                m4aT[:], ident_b[:],
                                start=False, stop=True, skip_group_check=True)
            smm[par] = tT.inc(last)

        def c_pv(par, hA, qb, e, msk):
            T.wait_ge(tS.h, e)
            if msk:
                T.wait_ge(tV.h, msk)
            T.wait_ge(tV.h, ev_done[par])
            last = None
            for op in mk_pv(par, hA, qb):
                last = op()
            pv_done[par] = tT.inc(last)
            return pv_done[par]

        pend = None
        for hp in range(16):
            hA = 2 * hp
            for qb in range(8):
                par = qb % 2
                c_scores(par, hA, qb)
                if pend is not None:
                    qbp, eP, mP, hAp = pend
                    parp = qbp % 2
                    pvm = c_pv(parp, hAp, qbp, eP, mP)
                    c_evac(parp, hAp, qbp, pvm)
                e, msk = c_exp(par, qb)
                pend = (qb, e, msk, hA)
        qbp, eP, mP, hAp = pend
        parp = qbp % 2
        pvm = c_pv(parp, hAp, qbp, eP, mP)
        c_evac(parp, hAp, qbp, pvm)

    barrier()
    if dbg:
        SY.dma_start(dbg_outs["dbg_atm"], atm.rearrange("p a b -> p (a b)")).then_inc(dDBG.h, 16)
        dDBG.n += 16

    with (
        nc.psum_tensor("pd0", [128, 1024], BF16) as pd0,
        nc.psum_tensor("pd1", [128, 1024], BF16) as pd1,
    ):
        if dbg:
            V.wait_ge(dDBG.h, dDBG.n)
        psDT = [pd0, pd1]
        last_copy = {0: 0, 1: 0}
        k = 0
        for qb in range(8):
            for gi in range(4):
                tp = k % 2
                k += 1
                T.wait_ge(tV.h, last_copy[tp])
                lastt = None
                for u in range(4):
                    lastt = T.transpose(
                        psDT[tp][:, 128 * u:128 * u + 128],
                        atm[:, qb, 128 * (4 * gi + u):128 * (4 * gi + u) + 128],
                        ident_b[:])
                t_tr = tT.inc(lastt)
                V.wait_ge(tT.h, t_tr)
                last_copy[tp] = vdo(lambda gi=gi, qb=qb, tp=tp: V.tensor_copy(
                    att[:, 4 * gi:4 * gi + 4, 128 * qb:128 * qb + 128],
                    psDT[tp][:, 0:512].rearrange("p (u j) -> p u j", u=4)))
    barrier()

    with (
        nc.psum_tensor("pw0", [128, 512], F32) as pw0,
        nc.psum_tensor("pw1", [128, 512], F32) as pw1,
        nc.psum_tensor("pw2", [128, 512], F32) as pw2,
        nc.psum_tensor("pw3", [128, 512], F32) as pw3,
        nc.psum_tensor("pw4", [128, 512], F32) as pw4,
        nc.psum_tensor("pw5", [128, 512], F32) as pw5,
        nc.psum_tensor("pw6", [128, 512], F32) as pw6,
        nc.psum_tensor("pw7", [128, 512], F32) as pw7,
    ):
        psW = [[pw0, pw1, pw2, pw3], [pw4, pw5, pw6, pw7]]
        if dbg:
            G.wait_ge(dDBG.h, dDBG.n)
        wobuf = [woh, woh2]
        ev_done = {0: 0, 1: 0}
        mm_ms = {}
        k = 0
        for q4 in range(4):
            if q4 + 1 < 4:
                if q4 >= 1:
                    G.wait_ge(tT.h, mm_ms[q4 - 1])
                G.wait_ge(dWO.h, dWO.n)
                G.dma_start(wobuf[(q4 + 1) % 2][:, :, :, :],
                            wo_pre[q4 + 1].transpose([1, 0, 2, 3])).then_inc(dWO.h, 16)
                dWO.n += 16
            for tl in range(2):
                pp = k % 2
                T.wait_ge(dWO.h, 16 * (q4 + 1))
                T.wait_ge(tV.h, ev_done[pp])
                last = None
                for oc in range(IFC):
                    for dmci in range(4):
                        last = T.matmul(psW[pp][dmci][:, :],
                                        wobuf[q4 % 2][:, oc, dmci, :],
                                        att[:, oc, 512 * tl:512 * tl + 512],
                                        start=(oc == 0), stop=(oc == IFC - 1))
                m = tT.inc(last)
                mm_ms[q4] = m
                V.wait_ge(tT.h, m)
                if k >= 2:
                    V.wait_ge(dAS2[pp].h, 16 * (k // 2))
                ev = None
                for dmci in range(4):
                    ev = vdo(lambda pp=pp, dmci=dmci: V.tensor_copy(
                        dstg[pp][:, dmci, :], psW[pp][dmci][:, :]))
                ev_done[pp] = ev
                SY.wait_ge(tV.h, ev)
                SY.dma_start(
                    attn_scr[4 * q4:4 * q4 + 4, :, 512 * tl:512 * tl + 512].transpose([1, 0, 2]),
                    dstg[pp][:, :, :]).then_inc(dAS2[pp].h, 16)
                dAS2[pp].n += 16
                k += 1

    barrier()
    if dbg:
        SY.dma_start(dbg_outs["dbg_att"], att.rearrange("p a b -> p (a b)")).then_inc(dDBG.h, 16)
        dDBG.n += 16

    S.preload_act_table(AF.Silu)
    with (
        nc.psum_tensor("pg0", [128, 512], F32) as pg0,
        nc.psum_tensor("pg1", [128, 512], F32) as pg1,
        nc.psum_tensor("pu0", [128, 512], F32) as pu0,
        nc.psum_tensor("pu1", [128, 512], F32) as pu1,
        nc.psum_tensor("pshA", [128, 512], F32) as pshA,
        nc.psum_tensor("pshB", [128, 512], F32) as pshB,
        nc.psum_tensor("pxc", [128, 1024], F32) as pxc,
    ):
        psg = [pg0, pg1]
        psu = [pu0, pu1]
        gd, silu_ms, stt_ms, stt_g, stt_u1 = {}, {}, {}, {}, {}
        lastsh = None
        T.wait_ge(dROW.h, sx_row_cnt)
        T.matmul(pxc[:, 0:512], cm128[:], sx2[:, 0:512], start=True, stop=True)
        t_xc = tT.inc(T.matmul(pxc[:, 512:1024], cm128[:], sx2[:, 512:1024],
                               start=True, stop=True))
        S.wait_ge(tT.h, t_xc)
        s_xc = sdo(lambda: S.copy(sxTb[:], pxc[:]))
        V.wait_ge(tS.h, s_xc)
        if dbg:
            V.wait_ge(dDBG.h, dDBG.n)
            G.wait_ge(dDBG.h, dDBG.n)
        V.wait_ge(dAS2[0].h, dAS2[0].n)
        V.wait_ge(dAS2[1].h, dAS2[1].n)
        G.wait_ge(tT.h, tT.n)
        for ic in range(ICN):
            par = ic % 2
            if ic >= 2:
                G.wait_ge(tT.h, gd[ic - 2])
            G.dma_start(gt[par][:, :, :], g_pre[ic]).then_inc(dG2[par].h, 16)
            G.dma_start(ut[par][:, :, :], u_pre[ic]).then_inc(dU2[par].h, 16)
            dG2[par].n += 16
            dU2[par].n += 16
            T.wait_ge(dG2[par].h, 16 * (ic // 2 + 1))
            T.wait_ge(dU2[par].h, 16 * (ic // 2 + 1))
            ms = {}
            for nm, ps_pair, wt in (("g", psg, gt[par]), ("u", psu, ut[par])):
                if ic >= 1:
                    ref = stt_g if nm == "g" else stt_u1
                    T.wait_ge(tV.h, ref[(ic - 1, 0)])
                    T.wait_ge(tV.h, ref[(ic - 1, 1)])
                for ci in range(IFC):
                    for sl in range(2):
                        mm = T.matmul(ps_pair[sl][:, :], wt[:, ci, :],
                                      xn[:, ci, 512 + 512 * sl:1024 + 512 * sl],
                                      start=(ci == 0), stop=(ci == IFC - 1))
                        if ci == IFC - 1:
                            ms[(nm, sl)] = tT.inc(mm)
            gd[ic] = ms[("u", 1)]
            for sl in range(2):
                sgs = sg[par][:, 512 * sl:512 * sl + 512]
                V.wait_ge(tT.h, ms[("g", sl)])
                stt_g[(ic, sl)] = vdo(lambda sgs=sgs, sl=sl: V.scalar_tensor_tensor(
                    sgs, psg[sl][:, :], 1.0, sxTb[:, 512 * sl:512 * sl + 512],
                    ALU.mult, ALU.add))
                S.wait_ge(tV.h, stt_g[(ic, sl)])
                silu_ms[(ic, sl)] = sdo(lambda ic=ic, sgs=sgs: S.activation(
                    sgs, sgs, AF.Silu, scale=gs_s[:, ic:ic + 1]))
            for sl in range(2):
                hts = h_t[:, ic, 512 * sl:512 * sl + 512]
                V.wait_ge(tT.h, ms[("u", sl)])
                stt_u1[(ic, sl)] = vdo(lambda hts=hts, sl=sl: V.scalar_tensor_tensor(
                    hts, psu[sl][:, :], 1.0, sxTb[:, 512 * sl:512 * sl + 512],
                    ALU.mult, ALU.add))
            for sl in range(2):
                sgs = sg[par][:, 512 * sl:512 * sl + 512]
                hts = h_t[:, ic, 512 * sl:512 * sl + 512]
                V.wait_ge(tS.h, silu_ms[(ic, sl)])
                stt_ms[(ic, sl)] = vdo(lambda ic=ic, hts=hts, sgs=sgs: V.scalar_tensor_tensor(
                    hts, hts, us_s[:, ic:ic + 1], sgs, ALU.mult, ALU.mult))
            for icd in ([ic - 1] if ic >= 1 else []) + ([ic] if ic == ICN - 1 else []):
                for sl in range(2):
                    T.wait_ge(tV.h, stt_ms[(icd, sl)])
                    lastsh = T.matmul((pshA if sl == 0 else pshB)[0:1, :], ones128[:],
                                      h_t[:, icd, 512 * sl:512 * sl + 512],
                                      start=(icd == 0), stop=(icd == ICN - 1))
        t_sh = tT.inc(lastsh)
        V.wait_ge(tT.h, t_sh)
        vdo(lambda: V.tensor_copy(sh2[0:1, 0:512], pshA[0:1, :]))
        vdo(lambda: V.tensor_copy(sh2[0:1, 512:1024], pshB[0:1, :]))
        vdo(lambda: V.tensor_sub(shlo[:, 0:512], pshA[0:1, :], sh2[0:1, 0:512]))
        v_shl = vdo(lambda: V.tensor_sub(shlo[:, 512:1024], pshB[0:1, :], sh2[0:1, 512:1024]))
        G.wait_ge(tV.h, v_shl)
        G.wait_ge(dROW.h, dROW.n)
        G.dma_start(sh2[1:2, :], shlo[:]).then_inc(dROW.h, 16)
        dROW.n += 16
        sh_row_cnt = dROW.n

    barrier()
    if dbg:
        SY.dma_start(dbg_outs["dbg_h"],
                     h_t[:, 0:2, :].rearrange("p a b -> p (a b)")).then_inc(dDBG.h, 16)
        SY.wait_ge(dROW.h, sh_row_cnt)
        SY.dma_start(dbg_outs["dbg_sh"], sh2[:]).then_inc(dDBG.h, 16)
        dDBG.n += 32

    with (
        nc.psum_tensor("pe0", [128, 512], F32) as pe0,
        nc.psum_tensor("pe1", [128, 512], F32) as pe1,
        nc.psum_tensor("pe2", [128, 512], F32) as pe2,
        nc.psum_tensor("pe3", [128, 512], F32) as pe3,
        nc.psum_tensor("pe4", [128, 512], F32) as pe4,
        nc.psum_tensor("pe5", [128, 512], F32) as pe5,
        nc.psum_tensor("pe6", [128, 512], F32) as pe6,
        nc.psum_tensor("pe7", [128, 512], F32) as pe7,
    ):
        psD = [[pe0, pe1, pe2, pe3], [pe4, pe5, pe6, pe7]]
        x_r16 = x_f.rearrange("(c p) s -> c p s", p=128)
        y_r = y.rearrange("(c p) t -> c p t", p=128)
        dgrp_ms = {}
        ev_ms = {0: 0, 1: 0}
        kq = 0
        G.wait_ge(tT.h, tT.n)
        for tl in range(2):
            for q4 in range(4):
                pp = kq % 2
                SY.wait_ge(tV.h, ev_ms[pp])
                SY.dma_start(
                    xres[pp][:, :, :],
                    x_r16[4 * q4:4 * q4 + 4, :, 512 + 512 * tl:1024 + 512 * tl].transpose([1, 0, 2]),
                ).then_inc(dXR2[pp].h, 16)
                dXR2[pp].n += 16
                if kq == 0:
                    SY.wait_ge(dAS2[0].h, dAS2[0].n)
                    SY.wait_ge(dAS2[1].h, dAS2[1].n)
                SY.dma_start(
                    arel[pp][:, :, :],
                    attn_scr[4 * q4:4 * q4 + 4, :, 512 * tl:512 * tl + 512].transpose([1, 0, 2]),
                ).then_inc(dAR2[pp].h, 16)
                dAR2[pp].n += 16
                T.wait_ge(tV.h, ev_ms[pp])
                T.wait_ge(dROW.h, sh_row_cnt)
                for ig in range(8):
                    kk = kq * 8 + ig
                    dpar = kk % 3
                    if kk >= 3:
                        G.wait_ge(tT.h, dgrp_ms[kk - 3])
                    if kk < 2:
                        G.wait_ge(dWC.h, dWC.n)
                    G.dma_start(
                        dtl[dpar][:, :, :, :],
                        wd_scr[8 * ig:8 * ig + 8, :, 4 * q4:4 * q4 + 4, :].transpose([1, 0, 2, 3]),
                    ).then_inc(dD2[dpar].h, 16)
                    dD2[dpar].n += 16
                    T.wait_ge(dD2[dpar].h, 16 * (kk // 3 + 1))
                    last = None
                    for ii in range(8):
                        ic = 8 * ig + ii
                        for dmci in range(4):
                            last = T.matmul(psD[pp][dmci][:, :],
                                            dtl[dpar][:, ii, dmci, :],
                                            h_t[:, ic, 512 * tl:512 * tl + 512],
                                            start=(ic == 0), stop=False,
                                            skip_group_check=True)
                    dgrp_ms[kk] = tT.inc(last)
                last = None
                for dmci in range(4):
                    last = T.matmul(psD[pp][dmci][:, :], cm128[:],
                                    sh2[:, 512 * tl:512 * tl + 512],
                                    start=False, stop=True, skip_group_check=True)
                m_d = tT.inc(last)
                V.wait_ge(tT.h, m_d)
                V.wait_ge(dXR2[pp].h, dXR2[pp].n)
                V.wait_ge(dAR2[pp].h, dAR2[pp].n)
                if kq >= 2:
                    V.wait_ge(dOUT2[pp].h, 16 * (kq // 2))
                ev = None
                for dmci in range(4):
                    dmc = 4 * q4 + dmci
                    vdo(lambda pp=pp, dmci=dmci, dmc=dmc: V.scalar_tensor_tensor(
                        outst[pp][:, dmci, :], psD[pp][dmci][:, :],
                        ds_s[:, dmc:dmc + 1], xres[pp][:, dmci, :],
                        ALU.mult, ALU.add))
                    ev = vdo(lambda pp=pp, dmci=dmci: V.tensor_add(
                        outst[pp][:, dmci, :], outst[pp][:, dmci, :], arel[pp][:, dmci, :]))
                ev_ms[pp] = ev
                SY.wait_ge(tV.h, ev)
                SY.dma_start(
                    y_r[4 * q4:4 * q4 + 4, :, 512 * tl:512 * tl + 512].transpose([1, 0, 2]),
                    outst[pp][:, :, :]).then_inc(dOUT2[pp].h, 16)
                dOUT2[pp].n += 16
                kq += 1
        SY.wait_ge(dOUT2[0].h, dOUT2[0].n)
        SY.wait_ge(dOUT2[1].h, dOUT2[1].n)
        if dbg:
            SY.wait_ge(dDBG.h, dDBG.n)

    return nc, stack


def _host_inputs(L0_attn_x, L0_attn_w_qkv, L0_attn_w_out, L0_attn_q_norm_w,
                 L0_attn_k_norm_w, L0_ffn_gate_w, L0_ffn_gate_scale,
                 L0_ffn_up_w, L0_ffn_up_scale, L0_ffn_down_w, L0_ffn_down_scale):
    f32 = np.float32
    x = np.asarray(L0_attn_x, f32)
    wqkv = np.ascontiguousarray(np.asarray(L0_attn_w_qkv, f32))
    wout = np.ascontiguousarray(np.asarray(L0_attn_w_out, f32))
    qnw = np.ascontiguousarray(np.asarray(L0_attn_q_norm_w, f32).reshape(1, HD))
    knw = np.ascontiguousarray(np.asarray(L0_attn_k_norm_w, f32).reshape(1, HD))
    gw = np.asarray(L0_ffn_gate_w, np.int32)
    uw = np.asarray(L0_ffn_up_w, np.int32)
    dw = np.asarray(L0_ffn_down_w, np.int32)

    wq_pre = np.ascontiguousarray(wqkv.T.reshape(IFC, 128, 3072))
    wo_pre = np.ascontiguousarray(
        wout.T.reshape(IFC, 128, 4, 4, 128).transpose(2, 0, 1, 3, 4))
    g_pre = np.ascontiguousarray(gw.reshape(ICN, 128, IFC, 128).transpose(0, 3, 2, 1))
    u_pre = np.ascontiguousarray(uw.reshape(ICN, 128, IFC, 128).transpose(0, 3, 2, 1))
    d_pre = np.ascontiguousarray(dw.reshape(IFC, 128, ICN, 128).transpose(2, 3, 0, 1))
    gs_pre = np.ascontiguousarray(np.asarray(L0_ffn_gate_scale, f32).reshape(ICN, 128).T)
    us_pre = np.ascontiguousarray(np.asarray(L0_ffn_up_scale, f32).reshape(ICN, 128).T)
    ds_pre = np.ascontiguousarray(np.asarray(L0_ffn_down_scale, f32).reshape(IFC, 128).T)

    in_maps = []
    for core in range(NCORES):
        b, half = core // 2, core % 2
        off = 1024 * half
        xc = np.zeros((SLOTS, DM), f32)
        lo = max(0, off - 512)
        s0 = lo - (off - 512)
        xc[s0:, :] = x[b, lo:off + 1024, :]
        x_fm = np.ascontiguousarray(xc.T)
        pos = (np.arange(SLOTS, dtype=f32) + (off - 512))
        invf = (10000.0 ** (-(np.arange(0, HD, 2, dtype=f32) / HD)))
        ang = pos[:, None] * invf[None, :]
        cos = np.concatenate([np.cos(ang), np.cos(ang)], axis=1)
        sin = np.concatenate([-np.sin(ang), np.sin(ang)], axis=1)
        cos_pre = np.ascontiguousarray(cos.reshape(TC, 128, HD).transpose(1, 0, 2))
        sin_pre = np.ascontiguousarray(sin.reshape(TC, 128, HD).transpose(1, 0, 2))
        expb_pre = np.full((128, 4), NEG if half == 0 else 0.0, f32)
        in_maps.append({
            "x_f": x_fm, "wq_pre": wq_pre, "wo_pre": wo_pre,
            "g_pre": g_pre, "u_pre": u_pre, "d_pre": d_pre,
            "gs_r": gs_pre, "us_r": us_pre, "ds_r": ds_pre,
            "cos_in": cos_pre, "sin_in": sin_pre,
            "qnw_in": qnw, "knw_in": knw, "expb_in": expb_pre,
        })
    return in_maps


def kernel(**inputs):
    in_maps = _host_inputs(**inputs)
    nc, stack = build_program()
    res = run_bass_kernel_spmd(nc, in_maps, core_ids=list(range(NCORES)))
    out = np.empty((BATCH, SEQ, DM), np.float32)
    for core in range(NCORES):
        b, half = core // 2, core % 2
        off = 1024 * half
        out[b, off:off + 1024, :] = res.results[core]["y"].T
    return out


# version 32
# speedup vs baseline: 1.1965x; 1.1965x over previous
import numpy as np
import ml_dtypes
import concourse.bass as bass
import concourse.mybir as mybir
from concourse.bass_utils import run_bass_kernel_spmd

F32 = mybir.dt.float32
BF16 = mybir.dt.bfloat16
I32 = mybir.dt.int32
AF = mybir.ActivationFunctionType
ALU = mybir.AluOpType
AX = mybir.AxisListType

DM = 2048
SEQ = 2048
BATCH = 4
NCORES = 8
SLOTS = 1536
TC = 12
IFC = 16
ICN = 64
NQH = 32
HD = 64
NEG = -30000.0 * 0.125

KB = 512


def _sz(dt):
    return {F32: 4, BF16: 2, I32: 4}[dt]


class Sem:
    def __init__(self, stack, nc, name):
        self.h = stack.enter_context(nc.semaphore(name))
        self.n = 0

    def inc(self, ins, k=1):
        ins.then_inc(self.h, k)
        self.n += k
        return self.n


def build_program(dbg=False):
    from contextlib import ExitStack
    nc = bass.Bass("TRN2", target_bir_lowering=False)

    x_f = nc.dram_tensor("x_f", [DM, SLOTS], F32, kind="ExternalInput").ap()
    wq_pre = nc.dram_tensor("wq_pre", [IFC, 128, 3072], F32, kind="ExternalInput").ap()
    wo_pre = nc.dram_tensor("wo_pre", [4, IFC, 128, 4, 128], F32, kind="ExternalInput").ap()
    g_pre = nc.dram_tensor("g_pre", [ICN, 128, IFC, 128], I32, kind="ExternalInput").ap()
    u_pre = nc.dram_tensor("u_pre", [ICN, 128, IFC, 128], I32, kind="ExternalInput").ap()
    d_pre = nc.dram_tensor("d_pre", [ICN, 128, IFC, 128], I32, kind="ExternalInput").ap()
    gs_r = nc.dram_tensor("gs_r", [128, ICN], F32, kind="ExternalInput").ap()
    us_r = nc.dram_tensor("us_r", [128, ICN], F32, kind="ExternalInput").ap()
    ds_r = nc.dram_tensor("ds_r", [128, IFC], F32, kind="ExternalInput").ap()
    cos_in = nc.dram_tensor("cos_in", [128, TC, HD], F32, kind="ExternalInput").ap()
    sin_in = nc.dram_tensor("sin_in", [128, TC, HD], F32, kind="ExternalInput").ap()
    qnw_in = nc.dram_tensor("qnw_in", [1, HD], F32, kind="ExternalInput").ap()
    knw_in = nc.dram_tensor("knw_in", [1, HD], F32, kind="ExternalInput").ap()
    expb_in = nc.dram_tensor("expb_in", [128, 4], F32, kind="ExternalInput").ap()
    y = nc.dram_tensor("y", [DM, 1024], F32, kind="ExternalOutput").ap()
    attn_scr = nc.dram_tensor("attn_scr", [IFC, 128, 1024], BF16, kind="Internal").ap()
    wd_scr = nc.dram_tensor("wd_scr", [ICN, 128, IFC, 128], BF16, kind="Internal").ap()
    dbg_outs = {}
    if dbg:
        for nm, shape, dt in [
            ("dbg_xn", [128, IFC * SLOTS], BF16),
            ("dbg_qf", [128, 16 * 1024], BF16),
            ("dbg_kf", [128, 8 * SLOTS], BF16),
            ("dbg_va", [128, TC * 520], BF16),
            ("dbg_atm", [128, 8 * 2048], BF16),
            ("dbg_att", [128, 16 * 1024], BF16),
            ("dbg_sx", [2, 1024], BF16),
            ("dbg_sh", [2, 1024], BF16),
            ("dbg_h", [128, 2 * 1024], BF16),
        ]:
            dbg_outs[nm] = nc.dram_tensor(nm, shape, dt, kind="ExternalOutput").ap()

    stack = ExitStack()

    ARENA_KB = 200
    AR = stack.enter_context(nc.sbuf_tensor("AR", [128, ARENA_KB * KB], BF16))

    def av(off_kb, shape, dt=BF16, p0=0):
        nfree = 1
        for s in shape[1:]:
            nfree *= s
        nb = nfree * _sz(dt)
        e0 = int(off_kb * KB)
        assert e0 * 2 == int(off_kb * 1024), off_kb
        ap = AR[p0:p0 + shape[0], e0: e0 + nb // 2]
        if dt != BF16:
            ap = ap.bitcast(dt)
        if len(shape) == 3:
            ap = ap.rearrange("p (a b) -> p a b", a=shape[1], b=shape[2])
        elif len(shape) == 4:
            ap = ap.rearrange("p (a b c) -> p a b c", a=shape[1], b=shape[2], c=shape[3])
        assert list(ap.shape) == list(shape), (ap.shape, shape)
        return ap

    sb = lambda name, shape, dt: stack.enter_context(nc.sbuf_tensor(name, shape, dt))
    ident_f = sb("ident_f", [128, 128], F32)
    ident_b = sb("ident_b", [128, 128], BF16)
    m0 = sb("m0", [128, 128], BF16)
    m4 = sb("m4", [128, 128], BF16)
    m0aT = sb("m0aT", [128, 128], BF16)
    m4aT = sb("m4aT", [128, 128], BF16)
    ones128 = sb("ones128", [128, 1], BF16)
    onesrow = sb("onesrow", [1, 128], BF16)
    negrow = sb("negrow", [1, 128], BF16)
    cm128 = sb("cm128", [2, 128], BF16)
    eps_ln = sb("eps_ln", [1, 1], F32)
    eps12 = sb("eps12", [12, 1], F32)
    eps_rms = sb("eps_rms", [128, 1], F32)
    expb = sb("expb", [128, 4], F32)
    gs_s = sb("gs_s", [128, ICN], F32)
    us_s = sb("us_s", [128, ICN], F32)
    ds_s = sb("ds_s", [128, IFC], F32)
    qnw = sb("qnw", [1, HD], F32)
    knw = sb("knw", [1, HD], F32)
    sxTb = sb("sxTb", [128, 1024], BF16)
    rcol = sb("rcol", [128, 1], F32)
    rcol4 = sb("rcol4", [128, 4], F32)
    ssq = sb("ssq", [128, 8], F32)
    srt = sb("srt", [128, 8], F32)
    rstd = sb("rstd", [128, 8], F32)

    xn = av(0, [128, IFC, SLOTS], BF16)
    xdb = [av(96, [128, SLOTS], F32), av(102, [128, SLOTS], F32)]
    sc12 = av(96, [12, 2, 128], F32)
    sct_a = av(97, [12, 128], F32)
    sct_b = av(97.5, [12, 128], F32)
    scb12 = av(98, [12, 2, 128], BF16)
    xsq = [av(108, [128, SLOTS], BF16), av(111, [128, SLOTS], BF16)]
    tmp_b = av(117, [128, SLOTS], BF16)
    rstdB = av(120, [128, SLOTS], BF16)
    mrsB = av(123, [128, SLOTS], BF16)
    cos_t = av(132, [128, TC, HD], F32)
    sin_t = av(135, [128, TC, HD], F32)
    r_a = av(138, [1, SLOTS], F32)
    r_b = av(144, [1, SLOTS], F32)
    r_c = av(150, [1, SLOTS], F32)
    r_d = av(156, [1, SLOTS], F32)
    rb_rstd = av(162, [1, SLOTS], BF16)
    rb_mrs = av(165, [1, SLOTS], BF16)
    cwq = av(176.5, [128, TC, HD], F32)
    swq = av(179.5, [128, TC, HD], F32)
    cwk = av(182.5, [128, TC, HD], F32)
    swk = av(185.5, [128, TC, HD], F32)
    wtmp = [av(188.5, [128, HD], F32), av(188.75, [128, HD], F32),
            av(189, [128, HD], F32), av(189.25, [128, HD], F32)]
    qnb = av(189.5, [1, HD], BF16)
    knb = av(189.75, [1, HD], BF16)
    sxlo = av(190, [1, 1024], BF16)
    wq_b = av(48, [128, IFC, 1536], BF16)
    q_f = av(96, [128, 16, 1024], BF16)
    k2v = av(128, [128, 4, 2, SLOTS], BF16)
    k_f8 = av(128, [128, 8, SLOTS], BF16)
    v_a = av(152, [128, TC, 520], BF16)
    b_q2 = av(164.5, [128, 512], F32)
    b_qs = av(166.5, [128, 512], F32)
    b_t1 = av(168.5, [128, 512], F32)
    b_t2 = av(170.5, [128, 512], F32)
    b_t3 = av(172.5, [128, 512], F32)
    b_qf2 = [av(174.5, [128, 512], F32), av(190.5, [128, 512], F32),
             av(188.5, [128, 512], F32), av(192.5, [128, 512], F32)]
    atm = av(48, [128, 8, 2048], BF16)
    pbufq = [[av(80, [128, 640], BF16), av(81.25, [128, 640], BF16)],
             [av(82.5, [128, 640], BF16), av(83.75, [128, 640], BF16)]]
    att = av(96, [128, 16, 1024], BF16)
    woh = av(180, [128, IFC, 4, 128], BF16)
    woh2 = av(48, [128, IFC, 4, 128], BF16)
    dstg = [av(64, [128, 4, 512], BF16), av(68, [128, 4, 512], BF16)]
    h_t = av(48, [128, ICN, 1024], BF16)
    gt = [av(176, [128, IFC, 128], BF16), av(180, [128, IFC, 128], BF16)]
    ut = [av(184, [128, IFC, 128], BF16), av(188, [128, IFC, 128], BF16)]
    sg = [av(192, [128, 1024], BF16), av(194, [128, 1024], BF16)]
    shlo = av(192, [1, 1024], BF16)
    sx2 = av(196, [2, 1024], BF16)
    sh2 = av(198, [2, 1024], BF16)
    dtl = [av(176, [128, 8, 4, 128], BF16), av(184, [128, 8, 4, 128], BF16),
           av(40, [128, 8, 4, 128], BF16)]
    xres = [av(0, [128, 4, 512], F32), av(8, [128, 4, 512], F32)]
    arel = [av(16, [128, 4, 512], BF16), av(20, [128, 4, 512], BF16)]
    outst = [av(24, [128, 4, 512], F32), av(32, [128, 4, 512], F32)]

    dX2 = [Sem(stack, nc, "dX0"), Sem(stack, nc, "dX1")]
    dTAB = Sem(stack, nc, "dTAB")
    dWQ4 = [Sem(stack, nc, "dWQa"), Sem(stack, nc, "dWQb"),
            Sem(stack, nc, "dWQc"), Sem(stack, nc, "dWQd")]
    dWO = Sem(stack, nc, "dWO")
    dG2 = [Sem(stack, nc, "dG0"), Sem(stack, nc, "dG1")]
    dU2 = [Sem(stack, nc, "dU0"), Sem(stack, nc, "dU1")]
    dD2 = [Sem(stack, nc, "dD0"), Sem(stack, nc, "dD1"), Sem(stack, nc, "dD2")]
    dXR2 = [Sem(stack, nc, "dXR0"), Sem(stack, nc, "dXR1")]
    dAS2 = [Sem(stack, nc, "dAS0"), Sem(stack, nc, "dAS1")]
    dAR2 = [Sem(stack, nc, "dAR0"), Sem(stack, nc, "dAR1")]
    dOUT2 = [Sem(stack, nc, "dOUT0"), Sem(stack, nc, "dOUT1")]
    dWC = Sem(stack, nc, "dWC")
    dROW = Sem(stack, nc, "dROW")
    dSC = Sem(stack, nc, "dSC")
    dDBG = Sem(stack, nc, "dDBG")
    tT = Sem(stack, nc, "tT"); tV = Sem(stack, nc, "tV")
    tS = Sem(stack, nc, "tS"); tG = Sem(stack, nc, "tG")

    T, V, S, G, SY = nc.tensor, nc.vector, nc.scalar, nc.gpsimd, nc.sync

    def vdo(make):
        V.wait_ge(tV.h, tV.n)
        return tV.inc(make())

    def sdo(make):
        S.wait_ge(tS.h, tS.n)
        return tS.inc(make())

    def gdo(make):
        G.wait_ge(tG.h, tG.n)
        return tG.inc(make())

    def barrier():
        counts = [(tT, tT.n), (tV, tV.n), (tS, tS.n), (tG, tG.n)]
        for e in (T, V, S, G, SY):
            for sm, n in counts:
                if n:
                    e.wait_ge(sm.h, n)

    x_r = x_f.rearrange("(c p) s -> c p s", p=128)
    SY.dma_start(xdb[0], x_r[0]).then_inc(dX2[0].h, 16)
    dX2[0].n += 16
    S.dma_start(xdb[1], x_r[1]).then_inc(dX2[1].h, 16)
    dX2[1].n += 16

    for dst, src in ((cos_t, cos_in), (sin_t, sin_in), (qnw[:], qnw_in),
                     (knw[:], knw_in), (gs_s[:], gs_r), (us_s[:], us_r),
                     (ds_s[:], ds_r), (expb[:], expb_in)):
        S.dma_start(dst, src).then_inc(dTAB.h, 16)
        dTAB.n += 16

    gdo(lambda: G.memset(ones128[:], 1.0))
    gdo(lambda: G.memset(onesrow[:], 1.0))
    gdo(lambda: G.memset(negrow[:], -1.0))
    gdo(lambda: G.memset(cm128[:], -128.0))
    gdo(lambda: G.memset(eps_ln[:], 1e-5))
    gdo(lambda: G.memset(eps12[:], 1e-5))
    gdo(lambda: G.memset(eps_rms[:], 1e-6))
    g_ones = tG.n
    gdo(lambda: G.memset(ident_f[:], 0.0))
    gdo(lambda: G.affine_select(out=ident_f[:], in_=ident_f[:], compare_op=ALU.not_equal,
                                fill=1.0, base=0, pattern=[[-1, 128]], channel_multiplier=1))
    gdo(lambda: G.memset(ident_b[:], 0.0))
    gdo(lambda: G.affine_select(out=ident_b[:], in_=ident_b[:], compare_op=ALU.not_equal,
                                fill=1.0, base=0, pattern=[[-1, 128]], channel_multiplier=1))
    gdo(lambda: G.memset(m0[:], 1.0))
    gdo(lambda: G.affine_select(out=m0[:], in_=m0[:], compare_op=ALU.is_gt,
                                fill=0.0, base=0, pattern=[[-1, 128]], channel_multiplier=1))
    gdo(lambda: G.memset(m4[:], 1.0))
    gdo(lambda: G.affine_select(out=m4[:], in_=m4[:], compare_op=ALU.is_ge,
                                fill=0.0, base=0, pattern=[[1, 128]], channel_multiplier=-1))
    gdo(lambda: G.memset(m0aT[:], 0.0))
    gdo(lambda: G.affine_select(out=m0aT[:], in_=m0aT[:], compare_op=ALU.is_gt,
                                fill=-30000.0, base=0, pattern=[[1, 128]], channel_multiplier=-1))
    gdo(lambda: G.memset(m4aT[:], 0.0))
    gdo(lambda: G.affine_select(out=m4aT[:], in_=m4aT[:], compare_op=ALU.is_ge,
                                fill=-30000.0, base=0, pattern=[[-1, 128]], channel_multiplier=1))
    G.wait_ge(dX2[0].h, 16 * 5)
    for j in range(4):
        G.dma_start(
            wq_b[:, 4 * j:4 * j + 4, :],
            wq_pre[4 * j:4 * j + 4, :, 0:1536].transpose([1, 0, 2]),
        ).then_inc(dWQ4[j].h, 16)
        dWQ4[j].n += 16
    G.wait_ge(dTAB.h, dTAB.n)
    gdo(lambda: G.tensor_copy(qnb, qnw[:]))
    g_rows = gdo(lambda: G.tensor_copy(knb, knw[:]))

    S.preload_act_table(AF.Sqrt)
    S.wait_ge(tG.h, g_ones)
    T.wait_ge(tG.h, g_ones)

    with (
        nc.psum_tensor("pa0", [128, 512], F32) as pa0,
        nc.psum_tensor("pa1", [128, 512], F32) as pa1,
        nc.psum_tensor("pa2", [128, 512], F32) as pa2,
        nc.psum_tensor("pa3", [128, 512], F32) as pa3,
        nc.psum_tensor("pa4", [128, 512], F32) as pa4,
        nc.psum_tensor("pa5", [128, 512], F32) as pa5,
        nc.psum_tensor("pa6", [128, 512], F32) as pa6,
        nc.psum_tensor("pa7", [128, 512], F32) as pa7,
    ):
        psx = [pa0, pa1, pa2]
        psq = [pa3, pa4, pa5]
        psb = [pa6, pa7]
        sq_done, cp_done, mm_done = {}, {}, {}
        t_ropemm = 0
        for ci in range(IFC):
            par = ci % 2
            eng = SY if par == 0 else S
            if ci >= 2:
                eng.wait_ge(tS.h, sq_done[ci - 2])
                eng.wait_ge(tV.h, cp_done[ci - 2])
                eng.dma_start(xdb[par], x_r[ci]).then_inc(dX2[par].h, 16)
                dX2[par].n += 16
            S.wait_ge(dX2[par].h, 16 * (ci // 2 + 1))
            if ci >= 2:
                S.wait_ge(tT.h, mm_done[ci - 2])
            sq_done[ci] = sdo(lambda par=par: S.activation(xsq[par], xdb[par], AF.Square))
            V.wait_ge(dX2[par].h, 16 * (ci // 2 + 1))
            cp_done[ci] = vdo(lambda ci=ci, par=par: V.tensor_copy(xn[:, ci, :], xdb[par]))
            T.wait_ge(tV.h, cp_done[ci])
            T.wait_ge(tS.h, sq_done[ci])
            last = None
            for sl in range(3):
                T.matmul(psx[sl][0:1, :], ones128[:], xn[:, ci, 512 * sl:512 * sl + 512],
                         start=(ci == 0), stop=(ci == IFC - 1))
                last = T.matmul(psq[sl][0:1, :], ones128[:],
                                xsq[par][:, 512 * sl:512 * sl + 512],
                                start=(ci == 0), stop=(ci == IFC - 1))
            mm_done[ci] = tT.inc(last)
            if ci == 8:
                T.wait_ge(tG.h, g_rows)
                T.matmul(psb[0][:, 0:HD], onesrow[:], qnb, start=True, stop=True)
                t_ropemm = tT.inc(T.matmul(psb[1][:, 0:HD], onesrow[:], knb,
                                           start=True, stop=True))
            if ci == 9:
                S.wait_ge(tT.h, t_ropemm)
                sdo(lambda: S.copy(wtmp[0], psb[0][:, 0:HD]))
                s_wtmp = sdo(lambda: S.copy(wtmp[1], psb[1][:, 0:HD]))
        G.wait_ge(tS.h, s_wtmp)
        gdo(lambda: G.tensor_copy(wtmp[2][:, 0:32], wtmp[0][:, 32:64]))
        gdo(lambda: G.tensor_copy(wtmp[2][:, 32:64], wtmp[0][:, 0:32]))
        gdo(lambda: G.tensor_copy(wtmp[3][:, 0:32], wtmp[1][:, 32:64]))
        gdo(lambda: G.tensor_copy(wtmp[3][:, 32:64], wtmp[1][:, 0:32]))
        G.wait_ge(dTAB.h, dTAB.n)
        gdo(lambda: G.tensor_mul(cwq, cos_t, wtmp[0].unsqueeze(1).broadcast_to([128, TC, HD])))
        gdo(lambda: G.tensor_mul(swq, sin_t, wtmp[2].unsqueeze(1).broadcast_to([128, TC, HD])))
        gdo(lambda: G.tensor_mul(cwk, cos_t, wtmp[1].unsqueeze(1).broadcast_to([128, TC, HD])))
        g_rope = gdo(lambda: G.tensor_mul(swk, sin_t,
                                          wtmp[3].unsqueeze(1).broadcast_to([128, TC, HD])))
        V.wait_ge(tT.h, mm_done[IFC - 1])
        v1 = 0
        for sl in range(3):
            vdo(lambda sl=sl: V.tensor_scalar_mul(r_a[:, 512 * sl:512 * sl + 512],
                                                  psx[sl][0:1, :], 1.0 / DM))
            v1 = vdo(lambda sl=sl: V.tensor_scalar_mul(r_b[:, 512 * sl:512 * sl + 512],
                                                       psq[sl][0:1, :], 1.0 / DM))
        SY.wait_ge(tV.h, v1)
        SY.wait_ge(tS.h, sq_done[IFC - 1])
        SY.wait_ge(dX2[0].h, dX2[0].n)
        mu12 = sc12[:, 0, :]
        m212 = sc12[:, 1, :]
        SY.dma_start(mu12, av(138, [1, 12, 128], F32)).then_inc(dSC.h, 16)
        SY.dma_start(m212, av(144, [1, 12, 128], F32)).then_inc(dSC.h, 16)
        dSC.n += 32
        V.wait_ge(dSC.h, dSC.n)
        vdo(lambda: V.tensor_mul(sct_a, mu12, mu12))
        v_var = vdo(lambda: V.scalar_tensor_tensor(sct_b, m212, 1.0, sct_a,
                                                   ALU.mult, ALU.subtract))
        S.wait_ge(tV.h, v_var)
        s2 = sdo(lambda: S.activation(sct_a, sct_b, AF.Sqrt, bias=eps12[:, 0:1]))
        V.wait_ge(tS.h, s2)
        vdo(lambda: V.reciprocal(sct_b, sct_a))
        vdo(lambda: V.tensor_mul(sct_a, mu12, sct_b))
        vdo(lambda: V.tensor_copy(scb12[:, 0, :], sct_b))
        v3 = vdo(lambda: V.tensor_copy(scb12[:, 1, :], sct_a))
        SY.wait_ge(tV.h, v3)
        SY.wait_ge(dSC.h, dSC.n)
        SY.dma_start(av(162, [1, 12, 128], BF16), scb12[:, 0, :]).then_inc(dSC.h, 16)
        SY.dma_start(av(165, [1, 12, 128], BF16), scb12[:, 1, :]).then_inc(dSC.h, 16)
        dSC.n += 32
        T.wait_ge(dSC.h, dSC.n)
        T.wait_ge(tS.h, s_wtmp)
        T.wait_ge(tV.h, v3)
        bk_r = [psb[0], psx[2], psq[1]]
        bk_m = [psb[1], psq[0], psq[2]]
        t_bc = 0
        for sl in range(3):
            T.matmul(bk_r[sl][:, :], onesrow[:],
                     rb_rstd[:, 512 * sl:512 * sl + 512], start=True, stop=True)
            t_bc = tT.inc(T.matmul(bk_m[sl][:, :], negrow[:],
                                   rb_mrs[:, 512 * sl:512 * sl + 512],
                                   start=True, stop=True))
        V.wait_ge(tT.h, t_bc)
        S.wait_ge(tT.h, t_bc)
        for sl in range(3):
            vdo(lambda sl=sl: V.tensor_copy(rstdB[:, 512 * sl:512 * sl + 512],
                                            bk_r[sl][:, :]))
            sdo(lambda sl=sl: S.copy(mrsB[:, 512 * sl:512 * sl + 512], bk_m[sl][:, :]))
        s_bc = tS.n
        V.wait_ge(tS.h, s_bc)
        last = None
        for ci in range(IFC):
            vdo(lambda ci=ci: V.tensor_mul(tmp_b, xn[:, ci, :], rstdB))
            xn_last = vdo(lambda ci=ci: V.tensor_add(xn[:, ci, :], tmp_b, mrsB))
            T.wait_ge(tV.h, xn_last)
            for sl in range(2):
                last = T.matmul(psx[sl][0:1, :], ones128[:],
                                xn[:, ci, 512 + 512 * sl:1024 + 512 * sl],
                                start=(ci == 0), stop=(ci == IFC - 1))
        t_sx = tT.inc(last)
        V.wait_ge(tT.h, t_sx)
        vdo(lambda: V.tensor_copy(sx2[0:1, 0:512], psx[0][0:1, :]))
        vdo(lambda: V.tensor_copy(sx2[0:1, 512:1024], psx[1][0:1, :]))
        vdo(lambda: V.tensor_sub(sxlo[:, 0:512], psx[0][0:1, :], sx2[0:1, 0:512]))
        v_sxl = vdo(lambda: V.tensor_sub(sxlo[:, 512:1024], psx[1][0:1, :], sx2[0:1, 512:1024]))
        G.wait_ge(tV.h, v_sxl)
        G.dma_start(sx2[1:2, :], sxlo[:]).then_inc(dROW.h, 16)
        dROW.n += 16
        sx_row_cnt = dROW.n

    barrier()
    gdo(lambda: G.memset(v_a.rearrange("p t (g e) -> p t g e", e=65)[:, :, :, 64:65], 1.0))
    if dbg:
        SY.wait_ge(dROW.h, sx_row_cnt)
        SY.dma_start(dbg_outs["dbg_xn"], xn.rearrange("p a b -> p (a b)")).then_inc(dDBG.h, 16)
        SY.dma_start(dbg_outs["dbg_sx"], sx2[:]).then_inc(dDBG.h, 16)
        dDBG.n += 32

    with (
        nc.psum_tensor("pb0", [128, 512], F32) as pb0,
        nc.psum_tensor("pb1", [128, 512], F32) as pb1,
        nc.psum_tensor("pb2", [128, 512], F32) as pb2,
        nc.psum_tensor("pb3", [128, 512], F32) as pb3,
        nc.psum_tensor("pb4", [128, 512], F32) as pb4,
        nc.psum_tensor("pb5", [128, 512], F32) as pb5,
        nc.psum_tensor("pT0", [128, 512], F32) as pT0,
        nc.psum_tensor("pT1", [128, 512], F32) as pT1,
    ):
        psB = [[pb0, pb1], [pb2, pb3], [pb4, pb5]]
        psT = [pT0, pT1]
        V.wait_ge(dROW.h, sx_row_cnt)
        ev_s, ev_v, mm_done = {}, {}, {}
        ptr = [0]
        psT_copy = {0: 0, 1: 0}
        pend = []
        qf_slot = [0]
        qf_rd = {}

        def emit_tr():
            while pend:
                s, t, tl, v_qf, slot = pend.pop(0)
                is_q = s <= 3
                qfbuf = b_qf2[slot]
                tp = ptr[0] % 2
                ptr[0] += 1
                T.wait_ge(tV.h, v_qf)
                T.wait_ge(tV.h, psT_copy[tp])
                lastt = None
                for u in range(4):
                    lastt = T.transpose(psT[tp][:, 128 * u:128 * u + 128],
                                        qfbuf[:, 128 * u:128 * u + 128], ident_f[:])
                t_tr = tT.inc(lastt)
                qf_rd[slot] = t_tr
                V.wait_ge(tT.h, t_tr)
                p3 = psT[tp][:, :].rearrange("p (u j) -> p u j", u=4)
                if is_q:
                    psT_copy[tp] = vdo(lambda s=s, tl=tl, p3=p3: V.tensor_copy(
                        q_f[:, 4 * s:4 * s + 4, 128 * tl:128 * tl + 128], p3))
                else:
                    vdo(lambda t=t, p3=p3: V.tensor_copy(
                        k2v[0:64, :, 0, 128 * t:128 * t + 128], p3[0:64, :, :]))
                    psT_copy[tp] = vdo(lambda t=t, p3=p3: V.tensor_copy(
                        k2v[64:128, :, 1, 128 * t:128 * t + 128], p3[64:128, :, :]))

        def emit_mm(grp, t, par, slices):
            for e_si, s in slices:
                if (e_si, par) in ev_s:
                    T.wait_ge(tS.h, ev_s[(e_si, par)])
                    T.wait_ge(tV.h, ev_v[(e_si, par)])
            last = None
            for ci in range(IFC):
                if ci % 4 == 0:
                    T.wait_ge(dWQ4[ci // 4].h, 16 * (grp + 1))
                for e_si, s in slices:
                    last = T.matmul(psB[e_si][par][:, :],
                                    xn[:, ci, 128 * t:128 * t + 128],
                                    wq_b[:, ci, 512 * (s - 3 * grp):512 * (s - 3 * grp) + 512],
                                    start=(ci == 0), stop=(ci == IFC - 1))
            mm_done[t] = tT.inc(last)

        def emit_evac(grp, t, par, slices):
            tl = t - 4
            for e_si, s in slices:
                ps = psB[e_si][par]
                if s <= 4:
                    is_q = s <= 3
                    cw, sw = (cwq, swq) if is_q else (cwk, swk)
                    S.wait_ge(tT.h, mm_done[t])
                    S.wait_ge(tV.h, ev_v.get("red", 0))
                    s_sq = sdo(lambda ps=ps: S.activation(b_q2, ps[:, :], AF.Square))
                    V.wait_ge(tS.h, s_sq)
                    V.wait_ge(tS.h, ev_s.get("srt", 0))
                    v_red = vdo(lambda: V.tensor_reduce(
                        ssq[:, 0:8].rearrange("p (g o) -> p g o", o=1),
                        b_q2.rearrange("p (g j) -> p g j", g=8), AX.X, ALU.add))
                    ev_v["red"] = v_red
                    S.wait_ge(tV.h, v_red)
                    S.wait_ge(tV.h, ev_v.get("rcp", 0))
                    s_srt = sdo(lambda: S.activation(srt[:, 0:8], ssq[:, 0:8], AF.Sqrt,
                                                     bias=eps_rms[:, 0:1], scale=1.0 / HD))
                    ev_s["srt"] = s_srt
                    V.wait_ge(tS.h, s_srt)
                    ev_v["rcp"] = vdo(lambda: V.reciprocal(rstd[:, 0:8], srt[:, 0:8]))
                    ps4d = ps[:, :].rearrange("p (g v j) -> p g v j", g=8, v=2)
                    cwb = cw[:, t, :].unsqueeze(1).broadcast_to([128, 8, HD])
                    swb0 = sw[:, t, 0:32].unsqueeze(1).broadcast_to([128, 8, 32])
                    swb1 = sw[:, t, 32:64].unsqueeze(1).broadcast_to([128, 8, 32])
                    t13 = b_t1.rearrange("p (g j) -> p g j", g=8)
                    t23 = b_t2.rearrange("p (g j) -> p g j", g=8)
                    t24 = b_t2.rearrange("p (g v j) -> p g v j", g=8, v=2)
                    t33 = b_t3.rearrange("p (g j) -> p g j", g=8)
                    V.wait_ge(tT.h, mm_done[t])
                    ev_pb = vdo(lambda ps=ps, t13=t13, cwb=cwb: V.tensor_mul(
                        t13, ps[:, :].rearrange("p (g j) -> p g j", g=8), cwb))
                    vdo(lambda t24=t24, ps4d=ps4d, swb0=swb0: V.tensor_mul(
                        t24[:, :, 0, :], ps4d[:, :, 1, :], swb0))
                    v_t2 = vdo(lambda t24=t24, ps4d=ps4d, swb1=swb1: V.tensor_mul(
                        t24[:, :, 1, :], ps4d[:, :, 0, :], swb1))
                    vdo(lambda t33=t33, t13=t13, t23=t23: V.tensor_add(t33, t13, t23))
                    rsb = rstd[:, 0:8].unsqueeze(2).broadcast_to([128, 8, HD])
                    qfbuf = b_qf2[qf_slot[0]]
                    qf3 = qfbuf.rearrange("p (g j) -> p g j", g=8)
                    V.wait_ge(tT.h, qf_rd.get(qf_slot[0], 0))
                    v_qf = vdo(lambda qf3=qf3, t33=t33, rsb=rsb: V.tensor_mul(qf3, t33, rsb))
                    ev_s[(e_si, par)] = s_sq
                    ev_v[(e_si, par)] = v_t2
                    pend.append((s, t, tl, v_qf, qf_slot[0]))
                    qf_slot[0] = (qf_slot[0] + 1) % 4
                else:
                    V.wait_ge(tT.h, mm_done[t])
                    vv = vdo(lambda t=t, ps=ps: V.tensor_copy(
                        v_a[:, t, :].rearrange("p (g e) -> p g e", e=65)[:, :, 0:64],
                        ps[:, :].rearrange("p (g j) -> p g j", g=8)))
                    ev_s[(e_si, par)] = tS.n
                    ev_v[(e_si, par)] = vv

        for ti, t in enumerate(range(4, 12)):
            emit_mm(0, t, ti % 2, [(0, 0), (1, 1), (2, 2)])
            emit_tr()
            emit_evac(0, t, ti % 2, [(0, 0), (1, 1), (2, 2)])
        emit_tr()
        G.wait_ge(tT.h, mm_done[11])
        for j in range(4):
            G.wait_ge(dWQ4[j].h, dWQ4[j].n)
            G.dma_start(
                wq_b[:, 4 * j:4 * j + 4, :],
                wq_pre[4 * j:4 * j + 4, :, 1536:3072].transpose([1, 0, 2]),
            ).then_inc(dWQ4[j].h, 16)
            dWQ4[j].n += 16
        for ti, t in enumerate(range(0, 12)):
            sl = [(0, 3), (1, 4), (2, 5)] if t >= 4 else [(1, 4), (2, 5)]
            emit_mm(1, t, ti % 2, sl)
            emit_tr()
            emit_evac(1, t, ti % 2, sl)
        emit_tr()

    barrier()
    G.wait_ge(dROW.h, dROW.n)
    G.dma_start(k2v[64:128, :, 0, :], k2v[0:64, :, 0, :]).then_inc(dROW.h, 16)
    G.dma_start(k2v[0:64, :, 1, :], k2v[64:128, :, 1, :]).then_inc(dROW.h, 16)
    dROW.n += 32
    krep_cnt = dROW.n
    if dbg:
        SY.wait_ge(dROW.h, krep_cnt)
        SY.dma_start(dbg_outs["dbg_qf"], q_f.rearrange("p a b -> p (a b)")).then_inc(dDBG.h, 16)
        SY.dma_start(dbg_outs["dbg_kf"], k_f8.rearrange("p a b -> p (a b)")).then_inc(dDBG.h, 16)
        SY.dma_start(dbg_outs["dbg_va"], v_a.rearrange("p a b -> p (a b)")).then_inc(dDBG.h, 16)
        dDBG.n += 48

    S.preload_act_table(AF.Exp)
    G.wait_ge(tT.h, tT.n)
    G.wait_ge(dWO.h, dWO.n)
    G.dma_start(woh[:, :, :, :], wo_pre[0].transpose([1, 0, 2, 3])).then_inc(dWO.h, 16)
    dWO.n += 16
    for j in range(8):
        G.dma_start(wd_scr[8 * j:8 * j + 8], d_pre[8 * j:8 * j + 8]).then_inc(dWC.h, 16)
        dWC.n += 16

    with (
        nc.psum_tensor("scmA", [128, 1024], F32) as scmA,
        nc.psum_tensor("sct0", [128, 512], F32) as sct0,
        nc.psum_tensor("scmB", [128, 1024], F32) as scmB,
        nc.psum_tensor("sct1", [128, 512], F32) as sct1,
        nc.psum_tensor("pvp0", [128, 512], F32) as pvp0,
        nc.psum_tensor("pvp1", [128, 512], F32) as pvp1,
    ):
        scm = [[scmA[:, 0:512], scmA[:, 512:1024]], [scmB[:, 0:512], scmB[:, 512:1024]]]
        sct = [sct0, sct1]
        pvp = [pvp0, pvp1]
        T.wait_ge(dROW.h, krep_cnt)

        exp_done = {0: 0, 1: 0}
        pv_done = {0: 0, 1: 0}
        ev_done = {0: 0, 1: 0}
        smm = {0: 0, 1: 0}

        def mk_scores(par, hA, qb):
            ops = []
            order = [(0, 0), (1, 0), (0, 1), (1, 1), (0, 2), (1, 2),
                     (0, 3), (1, 3), (0, 4), (1, 4)]
            order = sorted(order)
            for lane, o in order:
                hh = hA + lane
                kv = hh // 4
                qpair, qrow = hh // 2, 64 * (hh % 2)
                kb = qb + o
                lhs = k_f8[qrow:qrow + 64, kv, 128 * kb:128 * kb + 128]
                dst = (scm[par][lane][:, 128 * o:128 * o + 128] if o < 4
                       else sct[par][:, 128 * lane:128 * lane + 128])
                rhs = q_f[qrow:qrow + 64, qpair, 128 * qb:128 * qb + 128]
                start = (o == 0) if o < 4 else (lane == 0)
                stop = o in (1, 2, 3)
                ops.append(lambda dst=dst, lhs=lhs, rhs=rhs, start=start, stop=stop: T.matmul(
                    dst, lhs, rhs, start=start, stop=stop, skip_group_check=True))
            return ops

        def mk_pv(par, hA, qb):
            ops = []
            for lane in range(2):
                kv = (hA + lane) // 4
                pb = pbufq[par][lane]
                for o in range(5):
                    kb = qb + o
                    ops.append(lambda par=par, lane=lane, pb=pb, o=o, kb=kb, kv=kv:
                               T.matmul(pvp[par][:, 65 * lane:65 * lane + 65],
                                        pb[:, 128 * o:128 * o + 128],
                                        v_a[:, kb, 65 * kv:65 * kv + 65],
                                        start=(o == 0), stop=(o == 4),
                                        skip_group_check=True))
            return ops

        def mk_dummy():
            return [lambda: T.matmul(pvp[1][:, 0:64], ident_b[:], ident_b[:, 0:64],
                                     start=True, stop=True, skip_group_check=True)
                    for _ in range(10)]

        def c_exp(par, qb):
            S.wait_ge(tT.h, smm[par])
            S.wait_ge(tT.h, pv_done[par])
            pb2 = av(80 + 2.5 * par, [128, 2, 640], BF16)
            smv = [scmA, scmB][par][:, :].rearrange("p (l c) -> p l c", l=2)
            e = None
            if qb < 4:
                c = 128 * (4 - qb)
                e = sdo(lambda c=c: S.activation(
                    pb2[:, :, 0:c], smv[:, :, 0:c], AF.Exp,
                    bias=expb[:, qb:qb + 1], scale=0.125))
                if c < 512:
                    e = sdo(lambda c=c: S.activation(
                        pb2[:, :, c:512], smv[:, :, c:512], AF.Exp,
                        bias=0.0, scale=0.125))
            else:
                e = sdo(lambda: S.activation(
                    pb2[:, :, 0:512], smv, AF.Exp, bias=0.0, scale=0.125))
            e = sdo(lambda: S.activation(
                pb2[:, :, 512:640],
                sct[par][:, 0:256].rearrange("p (l c) -> p l c", l=2),
                AF.Exp, bias=0.0, scale=0.125))
            exp_done[par] = e
            return e, None

        def c_evac(par, hA, qb, pvm):
            V.wait_ge(tT.h, pvm)
            ev = None
            for lane in range(2):
                hh = hA + lane
                vdo(lambda par=par, lane=lane: V.reciprocal(
                    rcol[:, 0:1], pvp[par][:, 65 * lane + 64:65 * lane + 65]))
                ev = vdo(lambda par=par, lane=lane, qb=qb, hh=hh: V.tensor_scalar_mul(
                    atm[:, qb, 64 * hh:64 * hh + 64],
                    pvp[par][:, 65 * lane:65 * lane + 64], rcol[:, 0:1]))
            ev_done[par] = ev

        def c_scores(par, hA, qb):
            T.wait_ge(tS.h, exp_done[par])
            s_ops = mk_scores(par, hA, qb)
            last = None
            for lane in range(2):
                for i in range(5 * lane, 5 * lane + 5):
                    s_ops[i]()
                T.matmul(scm[par][lane][:, 0:128], m0aT[:], ident_b[:],
                         start=False, stop=True, skip_group_check=True)
                last = T.matmul(sct[par][:, 128 * lane:128 * lane + 128],
                                m4aT[:], ident_b[:],
                                start=False, stop=True, skip_group_check=True)
            smm[par] = tT.inc(last)

        def c_pv(par, hA, qb, e, msk):
            T.wait_ge(tS.h, e)
            if msk:
                T.wait_ge(tV.h, msk)
            T.wait_ge(tV.h, ev_done[par])
            last = None
            for op in mk_pv(par, hA, qb):
                last = op()
            pv_done[par] = tT.inc(last)
            return pv_done[par]

        pend = None
        for hp in range(16):
            hA = 2 * hp
            for qb in range(8):
                par = qb % 2
                c_scores(par, hA, qb)
                if pend is not None:
                    qbp, eP, mP, hAp = pend
                    parp = qbp % 2
                    pvm = c_pv(parp, hAp, qbp, eP, mP)
                    c_evac(parp, hAp, qbp, pvm)
                e, msk = c_exp(par, qb)
                pend = (qb, e, msk, hA)
        qbp, eP, mP, hAp = pend
        parp = qbp % 2
        pvm = c_pv(parp, hAp, qbp, eP, mP)
        c_evac(parp, hAp, qbp, pvm)

    barrier()
    if dbg:
        SY.dma_start(dbg_outs["dbg_atm"], atm.rearrange("p a b -> p (a b)")).then_inc(dDBG.h, 16)
        dDBG.n += 16

    with (
        nc.psum_tensor("pd0", [128, 1024], BF16) as pd0,
        nc.psum_tensor("pd1", [128, 1024], BF16) as pd1,
    ):
        if dbg:
            V.wait_ge(dDBG.h, dDBG.n)
        psDT = [pd0, pd1]
        last_copy = {0: 0, 1: 0}
        k = 0
        for qb in range(8):
            for gi in range(4):
                tp = k % 2
                k += 1
                T.wait_ge(tV.h, last_copy[tp])
                lastt = None
                for u in range(4):
                    lastt = T.transpose(
                        psDT[tp][:, 128 * u:128 * u + 128],
                        atm[:, qb, 128 * (4 * gi + u):128 * (4 * gi + u) + 128],
                        ident_b[:])
                t_tr = tT.inc(lastt)
                V.wait_ge(tT.h, t_tr)
                last_copy[tp] = vdo(lambda gi=gi, qb=qb, tp=tp: V.tensor_copy(
                    att[:, 4 * gi:4 * gi + 4, 128 * qb:128 * qb + 128],
                    psDT[tp][:, 0:512].rearrange("p (u j) -> p u j", u=4)))
    barrier()

    with (
        nc.psum_tensor("pw0", [128, 512], F32) as pw0,
        nc.psum_tensor("pw1", [128, 512], F32) as pw1,
        nc.psum_tensor("pw2", [128, 512], F32) as pw2,
        nc.psum_tensor("pw3", [128, 512], F32) as pw3,
        nc.psum_tensor("pw4", [128, 512], F32) as pw4,
        nc.psum_tensor("pw5", [128, 512], F32) as pw5,
        nc.psum_tensor("pw6", [128, 512], F32) as pw6,
        nc.psum_tensor("pw7", [128, 512], F32) as pw7,
    ):
        psW = [[pw0, pw1, pw2, pw3], [pw4, pw5, pw6, pw7]]
        if dbg:
            G.wait_ge(dDBG.h, dDBG.n)
        wobuf = [woh, woh2]
        ev_done = {0: 0, 1: 0}
        mm_ms = {}
        k = 0
        for q4 in range(4):
            if q4 + 1 < 4:
                if q4 >= 1:
                    G.wait_ge(tT.h, mm_ms[q4 - 1])
                G.wait_ge(dWO.h, dWO.n)
                G.dma_start(wobuf[(q4 + 1) % 2][:, :, :, :],
                            wo_pre[q4 + 1].transpose([1, 0, 2, 3])).then_inc(dWO.h, 16)
                dWO.n += 16
            for tl in range(2):
                pp = k % 2
                T.wait_ge(dWO.h, 16 * (q4 + 1))
                T.wait_ge(tV.h, ev_done[pp])
                last = None
                for oc in range(IFC):
                    for dmci in range(4):
                        last = T.matmul(psW[pp][dmci][:, :],
                                        wobuf[q4 % 2][:, oc, dmci, :],
                                        att[:, oc, 512 * tl:512 * tl + 512],
                                        start=(oc == 0), stop=(oc == IFC - 1))
                m = tT.inc(last)
                mm_ms[q4] = m
                V.wait_ge(tT.h, m)
                if k >= 2:
                    V.wait_ge(dAS2[pp].h, 16 * (k // 2))
                ev = None
                for dmci in range(4):
                    ev = vdo(lambda pp=pp, dmci=dmci: V.tensor_copy(
                        dstg[pp][:, dmci, :], psW[pp][dmci][:, :]))
                ev_done[pp] = ev
                SY.wait_ge(tV.h, ev)
                SY.dma_start(
                    attn_scr[4 * q4:4 * q4 + 4, :, 512 * tl:512 * tl + 512].transpose([1, 0, 2]),
                    dstg[pp][:, :, :]).then_inc(dAS2[pp].h, 16)
                dAS2[pp].n += 16
                k += 1

    barrier()
    if dbg:
        SY.dma_start(dbg_outs["dbg_att"], att.rearrange("p a b -> p (a b)")).then_inc(dDBG.h, 16)
        dDBG.n += 16

    S.preload_act_table(AF.Silu)
    with (
        nc.psum_tensor("pg0", [128, 512], F32) as pg0,
        nc.psum_tensor("pg1", [128, 512], F32) as pg1,
        nc.psum_tensor("pu0", [128, 512], F32) as pu0,
        nc.psum_tensor("pu1", [128, 512], F32) as pu1,
        nc.psum_tensor("pshA", [128, 512], F32) as pshA,
        nc.psum_tensor("pshB", [128, 512], F32) as pshB,
        nc.psum_tensor("pxc", [128, 1024], F32) as pxc,
    ):
        psg = [pg0, pg1]
        psu = [pu0, pu1]
        gd, silu_ms, stt_ms, stt_g, stt_u1 = {}, {}, {}, {}, {}
        lastsh = None
        T.wait_ge(dROW.h, sx_row_cnt)
        T.matmul(pxc[:, 0:512], cm128[:], sx2[:, 0:512], start=True, stop=True)
        t_xc = tT.inc(T.matmul(pxc[:, 512:1024], cm128[:], sx2[:, 512:1024],
                               start=True, stop=True))
        S.wait_ge(tT.h, t_xc)
        s_xc = sdo(lambda: S.copy(sxTb[:], pxc[:]))
        V.wait_ge(tS.h, s_xc)
        if dbg:
            V.wait_ge(dDBG.h, dDBG.n)
            G.wait_ge(dDBG.h, dDBG.n)
        V.wait_ge(dAS2[0].h, dAS2[0].n)
        V.wait_ge(dAS2[1].h, dAS2[1].n)
        G.wait_ge(tT.h, tT.n)
        for ic in range(ICN):
            par = ic % 2
            if ic >= 2:
                G.wait_ge(tT.h, gd[ic - 2])
            G.dma_start(gt[par][:, :, :], g_pre[ic]).then_inc(dG2[par].h, 16)
            G.dma_start(ut[par][:, :, :], u_pre[ic]).then_inc(dU2[par].h, 16)
            dG2[par].n += 16
            dU2[par].n += 16
            T.wait_ge(dG2[par].h, 16 * (ic // 2 + 1))
            T.wait_ge(dU2[par].h, 16 * (ic // 2 + 1))
            ms = {}
            for nm, ps_pair, wt in (("g", psg, gt[par]), ("u", psu, ut[par])):
                if ic >= 1:
                    ref = stt_g if nm == "g" else stt_u1
                    T.wait_ge(tV.h, ref[(ic - 1, 0)])
                    T.wait_ge(tV.h, ref[(ic - 1, 1)])
                for ci in range(IFC):
                    for sl in range(2):
                        mm = T.matmul(ps_pair[sl][:, :], wt[:, ci, :],
                                      xn[:, ci, 512 + 512 * sl:1024 + 512 * sl],
                                      start=(ci == 0), stop=(ci == IFC - 1))
                        if ci == IFC - 1:
                            ms[(nm, sl)] = tT.inc(mm)
            gd[ic] = ms[("u", 1)]
            for sl in range(2):
                sgs = sg[par][:, 512 * sl:512 * sl + 512]
                V.wait_ge(tT.h, ms[("g", sl)])
                stt_g[(ic, sl)] = vdo(lambda sgs=sgs, sl=sl: V.scalar_tensor_tensor(
                    sgs, psg[sl][:, :], 1.0, sxTb[:, 512 * sl:512 * sl + 512],
                    ALU.mult, ALU.add))
                S.wait_ge(tV.h, stt_g[(ic, sl)])
                silu_ms[(ic, sl)] = sdo(lambda ic=ic, sgs=sgs: S.activation(
                    sgs, sgs, AF.Silu, scale=gs_s[:, ic:ic + 1]))
            for sl in range(2):
                hts = h_t[:, ic, 512 * sl:512 * sl + 512]
                V.wait_ge(tT.h, ms[("u", sl)])
                stt_u1[(ic, sl)] = vdo(lambda hts=hts, sl=sl: V.scalar_tensor_tensor(
                    hts, psu[sl][:, :], 1.0, sxTb[:, 512 * sl:512 * sl + 512],
                    ALU.mult, ALU.add))
            for sl in range(2):
                sgs = sg[par][:, 512 * sl:512 * sl + 512]
                hts = h_t[:, ic, 512 * sl:512 * sl + 512]
                V.wait_ge(tS.h, silu_ms[(ic, sl)])
                stt_ms[(ic, sl)] = vdo(lambda ic=ic, hts=hts, sgs=sgs: V.scalar_tensor_tensor(
                    hts, hts, us_s[:, ic:ic + 1], sgs, ALU.mult, ALU.mult))
            for icd in ([ic - 1] if ic >= 1 else []) + ([ic] if ic == ICN - 1 else []):
                for sl in range(2):
                    T.wait_ge(tV.h, stt_ms[(icd, sl)])
                    lastsh = T.matmul((pshA if sl == 0 else pshB)[0:1, :], ones128[:],
                                      h_t[:, icd, 512 * sl:512 * sl + 512],
                                      start=(icd == 0), stop=(icd == ICN - 1))
        t_sh = tT.inc(lastsh)
        V.wait_ge(tT.h, t_sh)
        vdo(lambda: V.tensor_copy(sh2[0:1, 0:512], pshA[0:1, :]))
        vdo(lambda: V.tensor_copy(sh2[0:1, 512:1024], pshB[0:1, :]))
        vdo(lambda: V.tensor_sub(shlo[:, 0:512], pshA[0:1, :], sh2[0:1, 0:512]))
        v_shl = vdo(lambda: V.tensor_sub(shlo[:, 512:1024], pshB[0:1, :], sh2[0:1, 512:1024]))
        G.wait_ge(tV.h, v_shl)
        G.wait_ge(dROW.h, dROW.n)
        G.dma_start(sh2[1:2, :], shlo[:]).then_inc(dROW.h, 16)
        dROW.n += 16
        sh_row_cnt = dROW.n

    barrier()
    if dbg:
        SY.dma_start(dbg_outs["dbg_h"],
                     h_t[:, 0:2, :].rearrange("p a b -> p (a b)")).then_inc(dDBG.h, 16)
        SY.wait_ge(dROW.h, sh_row_cnt)
        SY.dma_start(dbg_outs["dbg_sh"], sh2[:]).then_inc(dDBG.h, 16)
        dDBG.n += 32

    with (
        nc.psum_tensor("pe0", [128, 512], F32) as pe0,
        nc.psum_tensor("pe1", [128, 512], F32) as pe1,
        nc.psum_tensor("pe2", [128, 512], F32) as pe2,
        nc.psum_tensor("pe3", [128, 512], F32) as pe3,
        nc.psum_tensor("pe4", [128, 512], F32) as pe4,
        nc.psum_tensor("pe5", [128, 512], F32) as pe5,
        nc.psum_tensor("pe6", [128, 512], F32) as pe6,
        nc.psum_tensor("pe7", [128, 512], F32) as pe7,
    ):
        psD = [[pe0, pe1, pe2, pe3], [pe4, pe5, pe6, pe7]]
        x_r16 = x_f.rearrange("(c p) s -> c p s", p=128)
        y_r = y.rearrange("(c p) t -> c p t", p=128)
        dgrp_ms = {}
        ev_ms = {0: 0, 1: 0}
        kq = 0
        G.wait_ge(tT.h, tT.n)
        for tl in range(2):
            for q4 in range(4):
                pp = kq % 2
                SY.wait_ge(tV.h, ev_ms[pp])
                SY.dma_start(
                    xres[pp][:, :, :],
                    x_r16[4 * q4:4 * q4 + 4, :, 512 + 512 * tl:1024 + 512 * tl].transpose([1, 0, 2]),
                ).then_inc(dXR2[pp].h, 16)
                dXR2[pp].n += 16
                if kq == 0:
                    SY.wait_ge(dAS2[0].h, dAS2[0].n)
                    SY.wait_ge(dAS2[1].h, dAS2[1].n)
                SY.dma_start(
                    arel[pp][:, :, :],
                    attn_scr[4 * q4:4 * q4 + 4, :, 512 * tl:512 * tl + 512].transpose([1, 0, 2]),
                ).then_inc(dAR2[pp].h, 16)
                dAR2[pp].n += 16
                T.wait_ge(tV.h, ev_ms[pp])
                T.wait_ge(dROW.h, sh_row_cnt)
                for ig in range(8):
                    kk = kq * 8 + ig
                    dpar = kk % 3
                    if kk >= 3:
                        G.wait_ge(tT.h, dgrp_ms[kk - 3])
                    if kk < 2:
                        G.wait_ge(dWC.h, dWC.n)
                    G.dma_start(
                        dtl[dpar][:, :, :, :],
                        wd_scr[8 * ig:8 * ig + 8, :, 4 * q4:4 * q4 + 4, :].transpose([1, 0, 2, 3]),
                    ).then_inc(dD2[dpar].h, 16)
                    dD2[dpar].n += 16
                    T.wait_ge(dD2[dpar].h, 16 * (kk // 3 + 1))
                    last = None
                    for ii in range(8):
                        ic = 8 * ig + ii
                        for dmci in range(4):
                            last = T.matmul(psD[pp][dmci][:, :],
                                            dtl[dpar][:, ii, dmci, :],
                                            h_t[:, ic, 512 * tl:512 * tl + 512],
                                            start=(ic == 0), stop=False,
                                            skip_group_check=True)
                    dgrp_ms[kk] = tT.inc(last)
                last = None
                for dmci in range(4):
                    last = T.matmul(psD[pp][dmci][:, :], cm128[:],
                                    sh2[:, 512 * tl:512 * tl + 512],
                                    start=False, stop=True, skip_group_check=True)
                m_d = tT.inc(last)
                V.wait_ge(tT.h, m_d)
                V.wait_ge(dXR2[pp].h, dXR2[pp].n)
                V.wait_ge(dAR2[pp].h, dAR2[pp].n)
                if kq >= 2:
                    V.wait_ge(dOUT2[pp].h, 16 * (kq // 2))
                ev = None
                for dmci in range(4):
                    dmc = 4 * q4 + dmci
                    vdo(lambda pp=pp, dmci=dmci, dmc=dmc: V.scalar_tensor_tensor(
                        outst[pp][:, dmci, :], psD[pp][dmci][:, :],
                        ds_s[:, dmc:dmc + 1], xres[pp][:, dmci, :],
                        ALU.mult, ALU.add))
                    ev = vdo(lambda pp=pp, dmci=dmci: V.tensor_add(
                        outst[pp][:, dmci, :], outst[pp][:, dmci, :], arel[pp][:, dmci, :]))
                ev_ms[pp] = ev
                SY.wait_ge(tV.h, ev)
                SY.dma_start(
                    y_r[4 * q4:4 * q4 + 4, :, 512 * tl:512 * tl + 512].transpose([1, 0, 2]),
                    outst[pp][:, :, :]).then_inc(dOUT2[pp].h, 16)
                dOUT2[pp].n += 16
                kq += 1
        SY.wait_ge(dOUT2[0].h, dOUT2[0].n)
        SY.wait_ge(dOUT2[1].h, dOUT2[1].n)
        if dbg:
            SY.wait_ge(dDBG.h, dDBG.n)

    return nc, stack


def _host_inputs(L0_attn_x, L0_attn_w_qkv, L0_attn_w_out, L0_attn_q_norm_w,
                 L0_attn_k_norm_w, L0_ffn_gate_w, L0_ffn_gate_scale,
                 L0_ffn_up_w, L0_ffn_up_scale, L0_ffn_down_w, L0_ffn_down_scale):
    f32 = np.float32
    x = np.asarray(L0_attn_x, f32)
    wqkv = np.ascontiguousarray(np.asarray(L0_attn_w_qkv, f32))
    wout = np.ascontiguousarray(np.asarray(L0_attn_w_out, f32))
    qnw = np.ascontiguousarray(np.asarray(L0_attn_q_norm_w, f32).reshape(1, HD))
    knw = np.ascontiguousarray(np.asarray(L0_attn_k_norm_w, f32).reshape(1, HD))
    gw = np.asarray(L0_ffn_gate_w, np.int32)
    uw = np.asarray(L0_ffn_up_w, np.int32)
    dw = np.asarray(L0_ffn_down_w, np.int32)

    wq_pre = np.ascontiguousarray(wqkv.T.reshape(IFC, 128, 3072))
    wo_pre = np.ascontiguousarray(
        wout.T.reshape(IFC, 128, 4, 4, 128).transpose(2, 0, 1, 3, 4))
    g_pre = np.ascontiguousarray(gw.reshape(ICN, 128, IFC, 128).transpose(0, 3, 2, 1))
    u_pre = np.ascontiguousarray(uw.reshape(ICN, 128, IFC, 128).transpose(0, 3, 2, 1))
    d_pre = np.ascontiguousarray(dw.reshape(IFC, 128, ICN, 128).transpose(2, 3, 0, 1))
    gs_pre = np.ascontiguousarray(np.asarray(L0_ffn_gate_scale, f32).reshape(ICN, 128).T)
    us_pre = np.ascontiguousarray(np.asarray(L0_ffn_up_scale, f32).reshape(ICN, 128).T)
    ds_pre = np.ascontiguousarray(np.asarray(L0_ffn_down_scale, f32).reshape(IFC, 128).T)

    in_maps = []
    for core in range(NCORES):
        b, half = core // 2, core % 2
        off = 1024 * half
        xc = np.zeros((SLOTS, DM), f32)
        lo = max(0, off - 512)
        s0 = lo - (off - 512)
        xc[s0:, :] = x[b, lo:off + 1024, :]
        x_fm = np.ascontiguousarray(xc.T)
        pos = (np.arange(SLOTS, dtype=f32) + (off - 512))
        invf = (10000.0 ** (-(np.arange(0, HD, 2, dtype=f32) / HD)))
        ang = pos[:, None] * invf[None, :]
        cos = np.concatenate([np.cos(ang), np.cos(ang)], axis=1)
        sin = np.concatenate([-np.sin(ang), np.sin(ang)], axis=1)
        cos_pre = np.ascontiguousarray(cos.reshape(TC, 128, HD).transpose(1, 0, 2))
        sin_pre = np.ascontiguousarray(sin.reshape(TC, 128, HD).transpose(1, 0, 2))
        expb_pre = np.full((128, 4), NEG if half == 0 else 0.0, f32)
        in_maps.append({
            "x_f": x_fm, "wq_pre": wq_pre, "wo_pre": wo_pre,
            "g_pre": g_pre, "u_pre": u_pre, "d_pre": d_pre,
            "gs_r": gs_pre, "us_r": us_pre, "ds_r": ds_pre,
            "cos_in": cos_pre, "sin_in": sin_pre,
            "qnw_in": qnw, "knw_in": knw, "expb_in": expb_pre,
        })
    return in_maps


def kernel(**inputs):
    in_maps = _host_inputs(**inputs)
    nc, stack = build_program()
    res = run_bass_kernel_spmd(nc, in_maps, core_ids=list(range(NCORES)))
    out = np.empty((BATCH, SEQ, DM), np.float32)
    for core in range(NCORES):
        b, half = core // 2, core % 2
        off = 1024 * half
        out[b, off:off + 1024, :] = res.results[core]["y"].T
    return out


# version 33
# speedup vs baseline: 1.2276x; 1.0260x over previous
import numpy as np
import ml_dtypes
import concourse.bass as bass
import concourse.mybir as mybir
from concourse.bass_utils import run_bass_kernel_spmd

F32 = mybir.dt.float32
BF16 = mybir.dt.bfloat16
I32 = mybir.dt.int32
AF = mybir.ActivationFunctionType
ALU = mybir.AluOpType
AX = mybir.AxisListType

DM = 2048
SEQ = 2048
BATCH = 4
NCORES = 8
SLOTS = 1536
TC = 12
IFC = 16
ICN = 64
NQH = 32
HD = 64
NEG = -30000.0 * 0.125

KB = 512


def _sz(dt):
    return {F32: 4, BF16: 2, I32: 4}[dt]


class Sem:
    def __init__(self, stack, nc, name):
        self.h = stack.enter_context(nc.semaphore(name))
        self.n = 0

    def inc(self, ins, k=1):
        ins.then_inc(self.h, k)
        self.n += k
        return self.n


def build_program(dbg=False):
    from contextlib import ExitStack
    nc = bass.Bass("TRN2", target_bir_lowering=False)

    x_f = nc.dram_tensor("x_f", [DM, SLOTS], F32, kind="ExternalInput").ap()
    wq_pre = nc.dram_tensor("wq_pre", [IFC, 128, 3072], F32, kind="ExternalInput").ap()
    wo_pre = nc.dram_tensor("wo_pre", [4, IFC, 128, 4, 128], F32, kind="ExternalInput").ap()
    g_pre = nc.dram_tensor("g_pre", [ICN, 128, IFC, 128], I32, kind="ExternalInput").ap()
    u_pre = nc.dram_tensor("u_pre", [ICN, 128, IFC, 128], I32, kind="ExternalInput").ap()
    d_pre = nc.dram_tensor("d_pre", [ICN, 128, IFC, 128], I32, kind="ExternalInput").ap()
    gs_r = nc.dram_tensor("gs_r", [128, ICN], F32, kind="ExternalInput").ap()
    us_r = nc.dram_tensor("us_r", [128, ICN], F32, kind="ExternalInput").ap()
    ds_r = nc.dram_tensor("ds_r", [128, IFC], F32, kind="ExternalInput").ap()
    cos_in = nc.dram_tensor("cos_in", [128, TC, HD], F32, kind="ExternalInput").ap()
    sin_in = nc.dram_tensor("sin_in", [128, TC, HD], F32, kind="ExternalInput").ap()
    qnw_in = nc.dram_tensor("qnw_in", [1, HD], F32, kind="ExternalInput").ap()
    knw_in = nc.dram_tensor("knw_in", [1, HD], F32, kind="ExternalInput").ap()
    expb_in = nc.dram_tensor("expb_in", [128, 4], F32, kind="ExternalInput").ap()
    y = nc.dram_tensor("y", [DM, 1024], F32, kind="ExternalOutput").ap()
    attn_scr = nc.dram_tensor("attn_scr", [IFC, 128, 1024], BF16, kind="Internal").ap()
    wd_scr = nc.dram_tensor("wd_scr", [ICN, 128, IFC, 128], BF16, kind="Internal").ap()
    dbg_outs = {}
    if dbg:
        for nm, shape, dt in [
            ("dbg_xn", [128, IFC * SLOTS], BF16),
            ("dbg_qf", [128, 16 * 1024], BF16),
            ("dbg_kf", [128, 8 * SLOTS], BF16),
            ("dbg_va", [128, TC * 520], BF16),
            ("dbg_atm", [128, 8 * 2048], BF16),
            ("dbg_att", [128, 16 * 1024], BF16),
            ("dbg_sx", [2, 1024], BF16),
            ("dbg_sh", [2, 1024], BF16),
            ("dbg_h", [128, 2 * 1024], BF16),
        ]:
            dbg_outs[nm] = nc.dram_tensor(nm, shape, dt, kind="ExternalOutput").ap()

    stack = ExitStack()

    ARENA_KB = 200
    AR = stack.enter_context(nc.sbuf_tensor("AR", [128, ARENA_KB * KB], BF16))

    def av(off_kb, shape, dt=BF16, p0=0):
        nfree = 1
        for s in shape[1:]:
            nfree *= s
        nb = nfree * _sz(dt)
        e0 = int(off_kb * KB)
        assert e0 * 2 == int(off_kb * 1024), off_kb
        ap = AR[p0:p0 + shape[0], e0: e0 + nb // 2]
        if dt != BF16:
            ap = ap.bitcast(dt)
        if len(shape) == 3:
            ap = ap.rearrange("p (a b) -> p a b", a=shape[1], b=shape[2])
        elif len(shape) == 4:
            ap = ap.rearrange("p (a b c) -> p a b c", a=shape[1], b=shape[2], c=shape[3])
        assert list(ap.shape) == list(shape), (ap.shape, shape)
        return ap

    sb = lambda name, shape, dt: stack.enter_context(nc.sbuf_tensor(name, shape, dt))
    ident_f = sb("ident_f", [128, 128], F32)
    ident_b = sb("ident_b", [128, 128], BF16)
    m0 = sb("m0", [128, 128], BF16)
    m4 = sb("m4", [128, 128], BF16)
    m0aT = sb("m0aT", [128, 128], BF16)
    m4aT = sb("m4aT", [128, 128], BF16)
    ones128 = sb("ones128", [128, 1], BF16)
    onesrow = sb("onesrow", [1, 128], BF16)
    negrow = sb("negrow", [1, 128], BF16)
    cm128 = sb("cm128", [2, 128], BF16)
    eps_ln = sb("eps_ln", [1, 1], F32)
    eps12 = sb("eps12", [12, 1], F32)
    eps_rms = sb("eps_rms", [128, 1], F32)
    expb = sb("expb", [128, 4], F32)
    gs_s = sb("gs_s", [128, ICN], F32)
    us_s = sb("us_s", [128, ICN], F32)
    ds_s = sb("ds_s", [128, IFC], F32)
    qnw = sb("qnw", [1, HD], F32)
    knw = sb("knw", [1, HD], F32)
    sxTb = sb("sxTb", [128, 1024], BF16)
    rcol = sb("rcol", [128, 1], F32)
    rcol4 = sb("rcol4", [128, 4], F32)
    ssq = sb("ssq", [128, 8], F32)
    srt = sb("srt", [128, 8], F32)
    rstd = sb("rstd", [128, 8], F32)

    xn = av(0, [128, IFC, SLOTS], BF16)
    xdb = [av(96, [128, SLOTS], F32), av(102, [128, SLOTS], F32)]
    sc12 = av(96, [12, 2, 128], F32)
    sct_a = av(97, [12, 128], F32)
    sct_b = av(97.5, [12, 128], F32)
    scb12 = av(98, [12, 2, 128], BF16)
    xsq = [av(108, [128, SLOTS], BF16), av(111, [128, SLOTS], BF16)]
    tmp_b = av(117, [128, SLOTS], BF16)
    rstdB = av(120, [128, SLOTS], BF16)
    mrsB = av(123, [128, SLOTS], BF16)
    cos_t = av(132, [128, TC, HD], F32)
    sin_t = av(135, [128, TC, HD], F32)
    r_a = av(138, [1, SLOTS], F32)
    r_b = av(144, [1, SLOTS], F32)
    r_c = av(150, [1, SLOTS], F32)
    r_d = av(156, [1, SLOTS], F32)
    rb_rstd = av(162, [1, SLOTS], BF16)
    rb_mrs = av(165, [1, SLOTS], BF16)
    cwq = av(176.5, [128, TC, HD], F32)
    swq = av(179.5, [128, TC, HD], F32)
    cwk = av(182.5, [128, TC, HD], F32)
    swk = av(185.5, [128, TC, HD], F32)
    wtmp = [av(188.5, [128, HD], F32), av(188.75, [128, HD], F32),
            av(189, [128, HD], F32), av(189.25, [128, HD], F32)]
    qnb = av(189.5, [1, HD], BF16)
    knb = av(189.75, [1, HD], BF16)
    sxlo = av(190, [1, 1024], BF16)
    wq_b = av(48, [128, IFC, 1536], BF16)
    q_f = av(96, [128, 16, 1024], BF16)
    k2v = av(128, [128, 4, 2, SLOTS], BF16)
    k_f8 = av(128, [128, 8, SLOTS], BF16)
    v_a = av(152, [128, TC, 520], BF16)
    b_q2 = av(164.5, [128, 512], F32)
    b_qs = av(166.5, [128, 512], F32)
    b_t1 = av(168.5, [128, 512], F32)
    b_t2 = av(170.5, [128, 512], F32)
    b_t3 = av(172.5, [128, 512], F32)
    b_qf2 = [av(174.5, [128, 512], F32), av(190.5, [128, 512], F32),
             av(188.5, [128, 512], F32), av(192.5, [128, 512], F32)]
    atm = av(48, [128, 8, 2048], BF16)
    pbufq = [[av(80, [128, 640], BF16), av(81.25, [128, 640], BF16)],
             [av(82.5, [128, 640], BF16), av(83.75, [128, 640], BF16)]]
    att = av(96, [128, 16, 1024], BF16)
    woh = av(180, [128, IFC, 4, 128], BF16)
    woh2 = av(48, [128, IFC, 4, 128], BF16)
    dstg = [av(64, [128, 4, 512], BF16), av(68, [128, 4, 512], BF16)]
    h_t = av(48, [128, ICN, 1024], BF16)
    gt = [av(176, [128, IFC, 128], BF16), av(180, [128, IFC, 128], BF16)]
    ut = [av(184, [128, IFC, 128], BF16), av(188, [128, IFC, 128], BF16)]
    sg = [av(192, [128, 1024], BF16), av(194, [128, 1024], BF16)]
    shlo = av(192, [1, 1024], BF16)
    sx2 = av(196, [2, 1024], BF16)
    sh2 = av(198, [2, 1024], BF16)
    dtl = [av(176, [128, 8, 4, 128], BF16), av(184, [128, 8, 4, 128], BF16),
           av(40, [128, 8, 4, 128], BF16)]
    xres = [av(0, [128, 4, 512], F32), av(8, [128, 4, 512], F32)]
    arel = [av(16, [128, 4, 512], BF16), av(20, [128, 4, 512], BF16)]
    outst = [av(24, [128, 4, 512], F32), av(32, [128, 4, 512], F32)]

    dX2 = [Sem(stack, nc, "dX0"), Sem(stack, nc, "dX1")]
    dTAB = Sem(stack, nc, "dTAB")
    dWQ4 = [Sem(stack, nc, "dWQa"), Sem(stack, nc, "dWQb"),
            Sem(stack, nc, "dWQc"), Sem(stack, nc, "dWQd")]
    dWO = Sem(stack, nc, "dWO")
    dG2 = [Sem(stack, nc, "dG0"), Sem(stack, nc, "dG1")]
    dU2 = [Sem(stack, nc, "dU0"), Sem(stack, nc, "dU1")]
    dD2 = [Sem(stack, nc, "dD0"), Sem(stack, nc, "dD1"), Sem(stack, nc, "dD2")]
    dXR2 = [Sem(stack, nc, "dXR0"), Sem(stack, nc, "dXR1")]
    dAS2 = [Sem(stack, nc, "dAS0"), Sem(stack, nc, "dAS1")]
    dAR2 = [Sem(stack, nc, "dAR0"), Sem(stack, nc, "dAR1")]
    dOUT2 = [Sem(stack, nc, "dOUT0"), Sem(stack, nc, "dOUT1")]
    dWC = Sem(stack, nc, "dWC")
    dROW = Sem(stack, nc, "dROW")
    dSC = Sem(stack, nc, "dSC")
    dDBG = Sem(stack, nc, "dDBG")
    tT = Sem(stack, nc, "tT"); tV = Sem(stack, nc, "tV")
    tS = Sem(stack, nc, "tS"); tG = Sem(stack, nc, "tG")

    T, V, S, G, SY = nc.tensor, nc.vector, nc.scalar, nc.gpsimd, nc.sync

    def vdo(make):
        V.wait_ge(tV.h, tV.n)
        return tV.inc(make())

    def sdo(make):
        S.wait_ge(tS.h, tS.n)
        return tS.inc(make())

    def gdo(make):
        G.wait_ge(tG.h, tG.n)
        return tG.inc(make())

    def barrier():
        counts = [(tT, tT.n), (tV, tV.n), (tS, tS.n), (tG, tG.n)]
        for e in (T, V, S, G, SY):
            for sm, n in counts:
                if n:
                    e.wait_ge(sm.h, n)

    x_r = x_f.rearrange("(c p) s -> c p s", p=128)
    SY.dma_start(xdb[0], x_r[0]).then_inc(dX2[0].h, 16)
    dX2[0].n += 16
    S.dma_start(xdb[1], x_r[1]).then_inc(dX2[1].h, 16)
    dX2[1].n += 16

    for dst, src in ((cos_t, cos_in), (sin_t, sin_in), (qnw[:], qnw_in),
                     (knw[:], knw_in), (gs_s[:], gs_r), (us_s[:], us_r),
                     (ds_s[:], ds_r), (expb[:], expb_in)):
        S.dma_start(dst, src).then_inc(dTAB.h, 16)
        dTAB.n += 16

    gdo(lambda: G.memset(ones128[:], 1.0))
    gdo(lambda: G.memset(onesrow[:], 1.0))
    gdo(lambda: G.memset(negrow[:], -1.0))
    gdo(lambda: G.memset(cm128[:], -128.0))
    gdo(lambda: G.memset(eps_ln[:], 1e-5))
    gdo(lambda: G.memset(eps12[:], 1e-5))
    gdo(lambda: G.memset(eps_rms[:], 1e-6))
    g_ones = tG.n
    gdo(lambda: G.memset(ident_f[:], 0.0))
    gdo(lambda: G.affine_select(out=ident_f[:], in_=ident_f[:], compare_op=ALU.not_equal,
                                fill=1.0, base=0, pattern=[[-1, 128]], channel_multiplier=1))
    gdo(lambda: G.memset(ident_b[:], 0.0))
    gdo(lambda: G.affine_select(out=ident_b[:], in_=ident_b[:], compare_op=ALU.not_equal,
                                fill=1.0, base=0, pattern=[[-1, 128]], channel_multiplier=1))
    gdo(lambda: G.memset(m0[:], 1.0))
    gdo(lambda: G.affine_select(out=m0[:], in_=m0[:], compare_op=ALU.is_gt,
                                fill=0.0, base=0, pattern=[[-1, 128]], channel_multiplier=1))
    gdo(lambda: G.memset(m4[:], 1.0))
    gdo(lambda: G.affine_select(out=m4[:], in_=m4[:], compare_op=ALU.is_ge,
                                fill=0.0, base=0, pattern=[[1, 128]], channel_multiplier=-1))
    gdo(lambda: G.memset(m0aT[:], 0.0))
    gdo(lambda: G.affine_select(out=m0aT[:], in_=m0aT[:], compare_op=ALU.is_gt,
                                fill=-30000.0, base=0, pattern=[[1, 128]], channel_multiplier=-1))
    gdo(lambda: G.memset(m4aT[:], 0.0))
    gdo(lambda: G.affine_select(out=m4aT[:], in_=m4aT[:], compare_op=ALU.is_ge,
                                fill=-30000.0, base=0, pattern=[[-1, 128]], channel_multiplier=1))
    G.wait_ge(dX2[0].h, 16 * 5)
    for j in range(4):
        G.dma_start(
            wq_b[:, 4 * j:4 * j + 4, :],
            wq_pre[4 * j:4 * j + 4, :, 0:1536].transpose([1, 0, 2]),
        ).then_inc(dWQ4[j].h, 16)
        dWQ4[j].n += 16
    G.wait_ge(dTAB.h, dTAB.n)
    gdo(lambda: G.tensor_copy(qnb, qnw[:]))
    g_rows = gdo(lambda: G.tensor_copy(knb, knw[:]))

    S.preload_act_table(AF.Sqrt)
    S.wait_ge(tG.h, g_ones)
    T.wait_ge(tG.h, g_ones)

    with (
        nc.psum_tensor("pa0", [128, 512], F32) as pa0,
        nc.psum_tensor("pa1", [128, 512], F32) as pa1,
        nc.psum_tensor("pa2", [128, 512], F32) as pa2,
        nc.psum_tensor("pa3", [128, 512], F32) as pa3,
        nc.psum_tensor("pa4", [128, 512], F32) as pa4,
        nc.psum_tensor("pa5", [128, 512], F32) as pa5,
        nc.psum_tensor("pa6", [128, 512], F32) as pa6,
        nc.psum_tensor("pa7", [128, 512], F32) as pa7,
    ):
        psx = [pa0, pa1, pa2]
        psq = [pa3, pa4, pa5]
        psb = [pa6, pa7]
        sq_done, cp_done, mm_done = {}, {}, {}
        t_ropemm = 0
        for ci in range(IFC):
            par = ci % 2
            eng = SY if par == 0 else S
            if ci >= 2:
                eng.wait_ge(tS.h, sq_done[ci - 2])
                eng.wait_ge(tV.h, cp_done[ci - 2])
                eng.dma_start(xdb[par], x_r[ci]).then_inc(dX2[par].h, 16)
                dX2[par].n += 16
            S.wait_ge(dX2[par].h, 16 * (ci // 2 + 1))
            if ci >= 2:
                S.wait_ge(tT.h, mm_done[ci - 2])
            sq_done[ci] = sdo(lambda par=par: S.activation(xsq[par], xdb[par], AF.Square))
            V.wait_ge(dX2[par].h, 16 * (ci // 2 + 1))
            cp_done[ci] = vdo(lambda ci=ci, par=par: V.tensor_copy(xn[:, ci, :], xdb[par]))
            T.wait_ge(tV.h, cp_done[ci])
            T.wait_ge(tS.h, sq_done[ci])
            last = None
            for sl in range(3):
                T.matmul(psx[sl][0:1, :], ones128[:], xn[:, ci, 512 * sl:512 * sl + 512],
                         start=(ci == 0), stop=(ci == IFC - 1))
                last = T.matmul(psq[sl][0:1, :], ones128[:],
                                xsq[par][:, 512 * sl:512 * sl + 512],
                                start=(ci == 0), stop=(ci == IFC - 1))
            mm_done[ci] = tT.inc(last)
            if ci == 8:
                T.wait_ge(tG.h, g_rows)
                T.matmul(psb[0][:, 0:HD], onesrow[:], qnb, start=True, stop=True)
                t_ropemm = tT.inc(T.matmul(psb[1][:, 0:HD], onesrow[:], knb,
                                           start=True, stop=True))
            if ci == 9:
                S.wait_ge(tT.h, t_ropemm)
                sdo(lambda: S.copy(wtmp[0], psb[0][:, 0:HD]))
                s_wtmp = sdo(lambda: S.copy(wtmp[1], psb[1][:, 0:HD]))
        G.wait_ge(tS.h, s_wtmp)
        gdo(lambda: G.tensor_copy(wtmp[2][:, 0:32], wtmp[0][:, 32:64]))
        gdo(lambda: G.tensor_copy(wtmp[2][:, 32:64], wtmp[0][:, 0:32]))
        gdo(lambda: G.tensor_copy(wtmp[3][:, 0:32], wtmp[1][:, 32:64]))
        gdo(lambda: G.tensor_copy(wtmp[3][:, 32:64], wtmp[1][:, 0:32]))
        G.wait_ge(dTAB.h, dTAB.n)
        gdo(lambda: G.tensor_mul(cwq, cos_t, wtmp[0].unsqueeze(1).broadcast_to([128, TC, HD])))
        gdo(lambda: G.tensor_mul(swq, sin_t, wtmp[2].unsqueeze(1).broadcast_to([128, TC, HD])))
        gdo(lambda: G.tensor_mul(cwk, cos_t, wtmp[1].unsqueeze(1).broadcast_to([128, TC, HD])))
        g_rope = gdo(lambda: G.tensor_mul(swk, sin_t,
                                          wtmp[3].unsqueeze(1).broadcast_to([128, TC, HD])))
        V.wait_ge(tT.h, mm_done[IFC - 1])
        v1 = 0
        for sl in range(3):
            vdo(lambda sl=sl: V.tensor_scalar_mul(r_a[:, 512 * sl:512 * sl + 512],
                                                  psx[sl][0:1, :], 1.0 / DM))
            v1 = vdo(lambda sl=sl: V.tensor_scalar_mul(r_b[:, 512 * sl:512 * sl + 512],
                                                       psq[sl][0:1, :], 1.0 / DM))
        SY.wait_ge(tV.h, v1)
        SY.wait_ge(tS.h, sq_done[IFC - 1])
        SY.wait_ge(dX2[0].h, dX2[0].n)
        mu12 = sc12[:, 0, :]
        m212 = sc12[:, 1, :]
        SY.dma_start(mu12, av(138, [1, 12, 128], F32)).then_inc(dSC.h, 16)
        SY.dma_start(m212, av(144, [1, 12, 128], F32)).then_inc(dSC.h, 16)
        dSC.n += 32
        V.wait_ge(dSC.h, dSC.n)
        vdo(lambda: V.tensor_mul(sct_a, mu12, mu12))
        v_var = vdo(lambda: V.scalar_tensor_tensor(sct_b, m212, 1.0, sct_a,
                                                   ALU.mult, ALU.subtract))
        S.wait_ge(tV.h, v_var)
        s2 = sdo(lambda: S.activation(sct_a, sct_b, AF.Sqrt, bias=eps12[:, 0:1]))
        V.wait_ge(tS.h, s2)
        vdo(lambda: V.reciprocal(sct_b, sct_a))
        vdo(lambda: V.tensor_mul(sct_a, mu12, sct_b))
        vdo(lambda: V.tensor_copy(scb12[:, 0, :], sct_b))
        v3 = vdo(lambda: V.tensor_copy(scb12[:, 1, :], sct_a))
        SY.wait_ge(tV.h, v3)
        SY.wait_ge(dSC.h, dSC.n)
        SY.dma_start(av(162, [1, 12, 128], BF16), scb12[:, 0, :]).then_inc(dSC.h, 16)
        SY.dma_start(av(165, [1, 12, 128], BF16), scb12[:, 1, :]).then_inc(dSC.h, 16)
        dSC.n += 32
        T.wait_ge(dSC.h, dSC.n)
        T.wait_ge(tS.h, s_wtmp)
        T.wait_ge(tV.h, v3)
        bk_r = [psb[0], psx[2], psq[1]]
        bk_m = [psb[1], psq[0], psq[2]]
        t_bc = 0
        for sl in range(3):
            T.matmul(bk_r[sl][:, :], onesrow[:],
                     rb_rstd[:, 512 * sl:512 * sl + 512], start=True, stop=True)
            t_bc = tT.inc(T.matmul(bk_m[sl][:, :], negrow[:],
                                   rb_mrs[:, 512 * sl:512 * sl + 512],
                                   start=True, stop=True))
        V.wait_ge(tT.h, t_bc)
        S.wait_ge(tT.h, t_bc)
        for sl in range(3):
            vdo(lambda sl=sl: V.tensor_copy(rstdB[:, 512 * sl:512 * sl + 512],
                                            bk_r[sl][:, :]))
            sdo(lambda sl=sl: S.copy(mrsB[:, 512 * sl:512 * sl + 512], bk_m[sl][:, :]))
        s_bc = tS.n
        V.wait_ge(tS.h, s_bc)
        last = None
        for ci in range(IFC):
            vdo(lambda ci=ci: V.tensor_mul(tmp_b, xn[:, ci, :], rstdB))
            xn_last = vdo(lambda ci=ci: V.tensor_add(xn[:, ci, :], tmp_b, mrsB))
            T.wait_ge(tV.h, xn_last)
            for sl in range(2):
                last = T.matmul(psx[sl][0:1, :], ones128[:],
                                xn[:, ci, 512 + 512 * sl:1024 + 512 * sl],
                                start=(ci == 0), stop=(ci == IFC - 1))
        t_sx = tT.inc(last)
        V.wait_ge(tT.h, t_sx)
        vdo(lambda: V.tensor_copy(sx2[0:1, 0:512], psx[0][0:1, :]))
        vdo(lambda: V.tensor_copy(sx2[0:1, 512:1024], psx[1][0:1, :]))
        vdo(lambda: V.tensor_sub(sxlo[:, 0:512], psx[0][0:1, :], sx2[0:1, 0:512]))
        v_sxl = vdo(lambda: V.tensor_sub(sxlo[:, 512:1024], psx[1][0:1, :], sx2[0:1, 512:1024]))
        G.wait_ge(tV.h, v_sxl)
        G.dma_start(sx2[1:2, :], sxlo[:]).then_inc(dROW.h, 16)
        dROW.n += 16
        sx_row_cnt = dROW.n

    barrier()
    gdo(lambda: G.memset(v_a.rearrange("p t (g e) -> p t g e", e=65)[:, :, :, 64:65], 1.0))
    if dbg:
        SY.wait_ge(dROW.h, sx_row_cnt)
        SY.dma_start(dbg_outs["dbg_xn"], xn.rearrange("p a b -> p (a b)")).then_inc(dDBG.h, 16)
        SY.dma_start(dbg_outs["dbg_sx"], sx2[:]).then_inc(dDBG.h, 16)
        dDBG.n += 32

    with (
        nc.psum_tensor("pb0", [128, 512], F32) as pb0,
        nc.psum_tensor("pb1", [128, 512], F32) as pb1,
        nc.psum_tensor("pb2", [128, 512], F32) as pb2,
        nc.psum_tensor("pb3", [128, 512], F32) as pb3,
        nc.psum_tensor("pb4", [128, 512], F32) as pb4,
        nc.psum_tensor("pb5", [128, 512], F32) as pb5,
        nc.psum_tensor("pT0", [128, 512], F32) as pT0,
        nc.psum_tensor("pT1", [128, 512], F32) as pT1,
    ):
        psB = [[pb0, pb1], [pb2, pb3], [pb4, pb5]]
        psT = [pT0, pT1]
        V.wait_ge(dROW.h, sx_row_cnt)
        ev_s, ev_v, mm_done = {}, {}, {}
        ptr = [0]
        psT_copy = {0: 0, 1: 0}
        pend = []
        qf_slot = [0]
        qf_rd = {}

        def emit_tr():
            while pend:
                s, t, tl, v_qf, slot = pend.pop(0)
                is_q = s <= 3
                qfbuf = b_qf2[slot]
                tp = ptr[0] % 2
                ptr[0] += 1
                T.wait_ge(tV.h, v_qf)
                T.wait_ge(tV.h, psT_copy[tp])
                lastt = None
                for u in range(4):
                    lastt = T.transpose(psT[tp][:, 128 * u:128 * u + 128],
                                        qfbuf[:, 128 * u:128 * u + 128], ident_f[:])
                t_tr = tT.inc(lastt)
                qf_rd[slot] = t_tr
                V.wait_ge(tT.h, t_tr)
                p3 = psT[tp][:, :].rearrange("p (u j) -> p u j", u=4)
                if is_q:
                    psT_copy[tp] = vdo(lambda s=s, tl=tl, p3=p3: V.tensor_copy(
                        q_f[:, 4 * s:4 * s + 4, 128 * tl:128 * tl + 128], p3))
                else:
                    vdo(lambda t=t, p3=p3: V.tensor_copy(
                        k2v[0:64, :, 0, 128 * t:128 * t + 128], p3[0:64, :, :]))
                    psT_copy[tp] = vdo(lambda t=t, p3=p3: V.tensor_copy(
                        k2v[64:128, :, 1, 128 * t:128 * t + 128], p3[64:128, :, :]))

        def emit_mm(grp, t, par, slices):
            for e_si, s in slices:
                if (e_si, par) in ev_s:
                    T.wait_ge(tS.h, ev_s[(e_si, par)])
                    T.wait_ge(tV.h, ev_v[(e_si, par)])
            last = None
            for ci in range(IFC):
                if ci % 4 == 0:
                    T.wait_ge(dWQ4[ci // 4].h, 16 * (grp + 1))
                for e_si, s in slices:
                    last = T.matmul(psB[e_si][par][:, :],
                                    xn[:, ci, 128 * t:128 * t + 128],
                                    wq_b[:, ci, 512 * (s - 3 * grp):512 * (s - 3 * grp) + 512],
                                    start=(ci == 0), stop=(ci == IFC - 1))
            mm_done[t] = tT.inc(last)

        def emit_evac(grp, t, par, slices):
            tl = t - 4
            for e_si, s in slices:
                ps = psB[e_si][par]
                if s <= 4:
                    is_q = s <= 3
                    cw, sw = (cwq, swq) if is_q else (cwk, swk)
                    S.wait_ge(tT.h, mm_done[t])
                    S.wait_ge(tV.h, ev_v.get("red", 0))
                    s_sq = sdo(lambda ps=ps: S.activation(b_q2, ps[:, :], AF.Square))
                    V.wait_ge(tS.h, s_sq)
                    V.wait_ge(tS.h, ev_s.get("srt", 0))
                    v_red = vdo(lambda: V.tensor_reduce(
                        ssq[:, 0:8].rearrange("p (g o) -> p g o", o=1),
                        b_q2.rearrange("p (g j) -> p g j", g=8), AX.X, ALU.add))
                    ev_v["red"] = v_red
                    S.wait_ge(tV.h, v_red)
                    S.wait_ge(tV.h, ev_v.get("rcp", 0))
                    s_srt = sdo(lambda: S.activation(srt[:, 0:8], ssq[:, 0:8], AF.Sqrt,
                                                     bias=eps_rms[:, 0:1], scale=1.0 / HD))
                    ev_s["srt"] = s_srt
                    V.wait_ge(tS.h, s_srt)
                    ev_v["rcp"] = vdo(lambda: V.reciprocal(rstd[:, 0:8], srt[:, 0:8]))
                    ps4d = ps[:, :].rearrange("p (g v j) -> p g v j", g=8, v=2)
                    cwb = cw[:, t, :].unsqueeze(1).broadcast_to([128, 8, HD])
                    swb0 = sw[:, t, 0:32].unsqueeze(1).broadcast_to([128, 8, 32])
                    swb1 = sw[:, t, 32:64].unsqueeze(1).broadcast_to([128, 8, 32])
                    t13 = b_t1.rearrange("p (g j) -> p g j", g=8)
                    t23 = b_t2.rearrange("p (g j) -> p g j", g=8)
                    t24 = b_t2.rearrange("p (g v j) -> p g v j", g=8, v=2)
                    t33 = b_t3.rearrange("p (g j) -> p g j", g=8)
                    V.wait_ge(tT.h, mm_done[t])
                    ev_pb = vdo(lambda ps=ps, t13=t13, cwb=cwb: V.tensor_mul(
                        t13, ps[:, :].rearrange("p (g j) -> p g j", g=8), cwb))
                    vdo(lambda t24=t24, ps4d=ps4d, swb0=swb0: V.tensor_mul(
                        t24[:, :, 0, :], ps4d[:, :, 1, :], swb0))
                    v_t2 = vdo(lambda t24=t24, ps4d=ps4d, swb1=swb1: V.tensor_mul(
                        t24[:, :, 1, :], ps4d[:, :, 0, :], swb1))
                    vdo(lambda t33=t33, t13=t13, t23=t23: V.tensor_add(t33, t13, t23))
                    rsb = rstd[:, 0:8].unsqueeze(2).broadcast_to([128, 8, HD])
                    qfbuf = b_qf2[qf_slot[0]]
                    qf3 = qfbuf.rearrange("p (g j) -> p g j", g=8)
                    V.wait_ge(tT.h, qf_rd.get(qf_slot[0], 0))
                    v_qf = vdo(lambda qf3=qf3, t33=t33, rsb=rsb: V.tensor_mul(qf3, t33, rsb))
                    ev_s[(e_si, par)] = s_sq
                    ev_v[(e_si, par)] = v_t2
                    pend.append((s, t, tl, v_qf, qf_slot[0]))
                    qf_slot[0] = (qf_slot[0] + 1) % 4
                else:
                    V.wait_ge(tT.h, mm_done[t])
                    vv = vdo(lambda t=t, ps=ps: V.tensor_copy(
                        v_a[:, t, :].rearrange("p (g e) -> p g e", e=65)[:, :, 0:64],
                        ps[:, :].rearrange("p (g j) -> p g j", g=8)))
                    ev_s[(e_si, par)] = tS.n
                    ev_v[(e_si, par)] = vv

        for ti, t in enumerate(range(4, 12)):
            emit_mm(0, t, ti % 2, [(0, 0), (1, 1), (2, 2)])
            emit_tr()
            emit_evac(0, t, ti % 2, [(0, 0), (1, 1), (2, 2)])
        emit_tr()
        G.wait_ge(tT.h, mm_done[11])
        for j in range(4):
            G.wait_ge(dWQ4[j].h, dWQ4[j].n)
            G.dma_start(
                wq_b[:, 4 * j:4 * j + 4, :],
                wq_pre[4 * j:4 * j + 4, :, 1536:3072].transpose([1, 0, 2]),
            ).then_inc(dWQ4[j].h, 16)
            dWQ4[j].n += 16
        for ti, t in enumerate(range(0, 12)):
            sl = [(0, 3), (1, 4), (2, 5)] if t >= 4 else [(1, 4), (2, 5)]
            emit_mm(1, t, ti % 2, sl)
            emit_tr()
            emit_evac(1, t, ti % 2, sl)
        emit_tr()

    barrier()
    G.wait_ge(dROW.h, dROW.n)
    G.dma_start(k2v[64:128, :, 0, :], k2v[0:64, :, 0, :]).then_inc(dROW.h, 16)
    G.dma_start(k2v[0:64, :, 1, :], k2v[64:128, :, 1, :]).then_inc(dROW.h, 16)
    dROW.n += 32
    krep_cnt = dROW.n
    if dbg:
        SY.wait_ge(dROW.h, krep_cnt)
        SY.dma_start(dbg_outs["dbg_qf"], q_f.rearrange("p a b -> p (a b)")).then_inc(dDBG.h, 16)
        SY.dma_start(dbg_outs["dbg_kf"], k_f8.rearrange("p a b -> p (a b)")).then_inc(dDBG.h, 16)
        SY.dma_start(dbg_outs["dbg_va"], v_a.rearrange("p a b -> p (a b)")).then_inc(dDBG.h, 16)
        dDBG.n += 48

    S.preload_act_table(AF.Exp)
    G.wait_ge(tT.h, tT.n)
    G.wait_ge(dWO.h, dWO.n)
    G.dma_start(woh[:, :, :, :], wo_pre[0].transpose([1, 0, 2, 3])).then_inc(dWO.h, 16)
    dWO.n += 16
    for j in range(8):
        G.dma_start(wd_scr[8 * j:8 * j + 8], d_pre[8 * j:8 * j + 8]).then_inc(dWC.h, 16)
        dWC.n += 16

    with (
        nc.psum_tensor("scmA", [128, 1024], F32) as scmA,
        nc.psum_tensor("sct0", [128, 512], F32) as sct0,
        nc.psum_tensor("scmB", [128, 1024], F32) as scmB,
        nc.psum_tensor("sct1", [128, 512], F32) as sct1,
        nc.psum_tensor("pvp0", [128, 512], F32) as pvp0,
        nc.psum_tensor("pvp1", [128, 512], F32) as pvp1,
    ):
        scm = [[scmA[:, 0:512], scmA[:, 512:1024]], [scmB[:, 0:512], scmB[:, 512:1024]]]
        sct = [sct0, sct1]
        pvp = [pvp0, pvp1]
        T.wait_ge(dROW.h, krep_cnt)

        exp_done = {0: 0, 1: 0}
        pv_done = {0: 0, 1: 0}
        ev_done = {0: 0, 1: 0}
        smm = {0: 0, 1: 0}

        def mk_scores(par, hA, qb):
            ops = []
            order = [(0, 4), (0, 0), (1, 0), (0, 1), (1, 1), (0, 2), (1, 2),
                     (0, 3), (1, 3), (1, 4)]
            for lane, o in order:
                hh = hA + lane
                kv = hh // 4
                qpair, qrow = hh // 2, 64 * (hh % 2)
                kb = qb + o
                lhs = k_f8[qrow:qrow + 64, kv, 128 * kb:128 * kb + 128]
                dst = (scm[par][lane][:, 128 * o:128 * o + 128] if o < 4
                       else sct[par][:, 128 * lane:128 * lane + 128])
                rhs = q_f[qrow:qrow + 64, qpair, 128 * qb:128 * qb + 128]
                ops.append(lambda dst=dst, lhs=lhs, rhs=rhs, qrow=qrow: T.matmul(
                    dst, lhs, rhs, start=False, stop=True,
                    tile_position=(qrow, 0), skip_group_check=True))
            return ops

        def mk_pv(par, hA, qb):
            ops = []
            for lane in range(2):
                kv = (hA + lane) // 4
                pb = pbufq[par][lane]
                for o in range(5):
                    kb = qb + o
                    ops.append(lambda par=par, lane=lane, pb=pb, o=o, kb=kb, kv=kv:
                               T.matmul(pvp[par][:, 65 * lane:65 * lane + 65],
                                        pb[:, 128 * o:128 * o + 128],
                                        v_a[:, kb, 65 * kv:65 * kv + 65],
                                        start=(o == 0), stop=(o == 4),
                                        skip_group_check=True))
            return ops

        def mk_dummy():
            return [lambda: T.matmul(pvp[1][:, 0:64], ident_b[:], ident_b[:, 0:64],
                                     start=True, stop=True, skip_group_check=True)
                    for _ in range(10)]

        def c_exp(par, qb):
            S.wait_ge(tT.h, smm[par])
            S.wait_ge(tT.h, pv_done[par])
            pb2 = av(80 + 2.5 * par, [128, 2, 640], BF16)
            smv = [scmA, scmB][par][:, :].rearrange("p (l c) -> p l c", l=2)
            e = None
            if qb < 4:
                c = 128 * (4 - qb)
                e = sdo(lambda c=c: S.activation(
                    pb2[:, :, 0:c], smv[:, :, 0:c], AF.Exp,
                    bias=expb[:, qb:qb + 1], scale=0.125))
                if c < 512:
                    e = sdo(lambda c=c: S.activation(
                        pb2[:, :, c:512], smv[:, :, c:512], AF.Exp,
                        bias=0.0, scale=0.125))
            else:
                e = sdo(lambda: S.activation(
                    pb2[:, :, 0:512], smv, AF.Exp, bias=0.0, scale=0.125))
            e = sdo(lambda: S.activation(
                pb2[:, :, 512:640],
                sct[par][:, 0:256].rearrange("p (l c) -> p l c", l=2),
                AF.Exp, bias=0.0, scale=0.125))
            exp_done[par] = e
            return e, None

        def c_evac(par, hA, qb, pvm):
            V.wait_ge(tT.h, pvm)
            ev = None
            for lane in range(2):
                hh = hA + lane
                vdo(lambda par=par, lane=lane: V.reciprocal(
                    rcol[:, 0:1], pvp[par][:, 65 * lane + 64:65 * lane + 65]))
                ev = vdo(lambda par=par, lane=lane, qb=qb, hh=hh: V.tensor_scalar_mul(
                    atm[:, qb, 64 * hh:64 * hh + 64],
                    pvp[par][:, 65 * lane:65 * lane + 64], rcol[:, 0:1]))
            ev_done[par] = ev

        def c_scores(par, hA, qb):
            T.wait_ge(tS.h, exp_done[par])
            T.matmul(scm[par][0][:, 0:128], m0aT[:], ident_b[:],
                     start=True, stop=False, skip_group_check=True)
            T.matmul(scm[par][1][:, 0:128], m0aT[:], ident_b[:],
                     start=True, stop=False, skip_group_check=True)
            T.matmul(sct[par][:, 0:128], m4aT[:], ident_b[:],
                     start=True, stop=False, skip_group_check=True)
            T.matmul(sct[par][:, 128:256], m4aT[:], ident_b[:],
                     start=False, stop=False, skip_group_check=True)
            last = None
            for op in mk_scores(par, hA, qb):
                last = op()
            smm[par] = tT.inc(last)

        def c_pv(par, hA, qb, e, msk):
            T.wait_ge(tS.h, e)
            if msk:
                T.wait_ge(tV.h, msk)
            T.wait_ge(tV.h, ev_done[par])
            last = None
            for op in mk_pv(par, hA, qb):
                last = op()
            pv_done[par] = tT.inc(last)
            return pv_done[par]

        pend = None
        for hp in range(16):
            hA = 2 * hp
            for qb in range(8):
                par = qb % 2
                c_scores(par, hA, qb)
                if pend is not None:
                    qbp, eP, mP, hAp = pend
                    parp = qbp % 2
                    pvm = c_pv(parp, hAp, qbp, eP, mP)
                    c_evac(parp, hAp, qbp, pvm)
                e, msk = c_exp(par, qb)
                pend = (qb, e, msk, hA)
        qbp, eP, mP, hAp = pend
        parp = qbp % 2
        pvm = c_pv(parp, hAp, qbp, eP, mP)
        c_evac(parp, hAp, qbp, pvm)

    barrier()
    if dbg:
        SY.dma_start(dbg_outs["dbg_atm"], atm.rearrange("p a b -> p (a b)")).then_inc(dDBG.h, 16)
        dDBG.n += 16

    with (
        nc.psum_tensor("pd0", [128, 1024], BF16) as pd0,
        nc.psum_tensor("pd1", [128, 1024], BF16) as pd1,
    ):
        if dbg:
            V.wait_ge(dDBG.h, dDBG.n)
        psDT = [pd0, pd1]
        last_copy = {0: 0, 1: 0}
        k = 0
        for qb in range(8):
            for gi in range(4):
                tp = k % 2
                k += 1
                T.wait_ge(tV.h, last_copy[tp])
                lastt = None
                for u in range(4):
                    lastt = T.transpose(
                        psDT[tp][:, 128 * u:128 * u + 128],
                        atm[:, qb, 128 * (4 * gi + u):128 * (4 * gi + u) + 128],
                        ident_b[:])
                t_tr = tT.inc(lastt)
                V.wait_ge(tT.h, t_tr)
                last_copy[tp] = vdo(lambda gi=gi, qb=qb, tp=tp: V.tensor_copy(
                    att[:, 4 * gi:4 * gi + 4, 128 * qb:128 * qb + 128],
                    psDT[tp][:, 0:512].rearrange("p (u j) -> p u j", u=4)))
    barrier()

    with (
        nc.psum_tensor("pw0", [128, 512], F32) as pw0,
        nc.psum_tensor("pw1", [128, 512], F32) as pw1,
        nc.psum_tensor("pw2", [128, 512], F32) as pw2,
        nc.psum_tensor("pw3", [128, 512], F32) as pw3,
        nc.psum_tensor("pw4", [128, 512], F32) as pw4,
        nc.psum_tensor("pw5", [128, 512], F32) as pw5,
        nc.psum_tensor("pw6", [128, 512], F32) as pw6,
        nc.psum_tensor("pw7", [128, 512], F32) as pw7,
    ):
        psW = [[pw0, pw1, pw2, pw3], [pw4, pw5, pw6, pw7]]
        if dbg:
            G.wait_ge(dDBG.h, dDBG.n)
        wobuf = [woh, woh2]
        ev_done = {0: 0, 1: 0}
        mm_ms = {}
        k = 0
        for q4 in range(4):
            if q4 + 1 < 4:
                if q4 >= 1:
                    G.wait_ge(tT.h, mm_ms[q4 - 1])
                G.wait_ge(dWO.h, dWO.n)
                G.dma_start(wobuf[(q4 + 1) % 2][:, :, :, :],
                            wo_pre[q4 + 1].transpose([1, 0, 2, 3])).then_inc(dWO.h, 16)
                dWO.n += 16
            for tl in range(2):
                pp = k % 2
                T.wait_ge(dWO.h, 16 * (q4 + 1))
                T.wait_ge(tV.h, ev_done[pp])
                last = None
                for oc in range(IFC):
                    for dmci in range(4):
                        last = T.matmul(psW[pp][dmci][:, :],
                                        wobuf[q4 % 2][:, oc, dmci, :],
                                        att[:, oc, 512 * tl:512 * tl + 512],
                                        start=(oc == 0), stop=(oc == IFC - 1))
                m = tT.inc(last)
                mm_ms[q4] = m
                V.wait_ge(tT.h, m)
                if k >= 2:
                    V.wait_ge(dAS2[pp].h, 16 * (k // 2))
                ev = None
                for dmci in range(4):
                    ev = vdo(lambda pp=pp, dmci=dmci: V.tensor_copy(
                        dstg[pp][:, dmci, :], psW[pp][dmci][:, :]))
                ev_done[pp] = ev
                SY.wait_ge(tV.h, ev)
                SY.dma_start(
                    attn_scr[4 * q4:4 * q4 + 4, :, 512 * tl:512 * tl + 512].transpose([1, 0, 2]),
                    dstg[pp][:, :, :]).then_inc(dAS2[pp].h, 16)
                dAS2[pp].n += 16
                k += 1

    barrier()
    if dbg:
        SY.dma_start(dbg_outs["dbg_att"], att.rearrange("p a b -> p (a b)")).then_inc(dDBG.h, 16)
        dDBG.n += 16

    S.preload_act_table(AF.Silu)
    with (
        nc.psum_tensor("pg0", [128, 512], F32) as pg0,
        nc.psum_tensor("pg1", [128, 512], F32) as pg1,
        nc.psum_tensor("pu0", [128, 512], F32) as pu0,
        nc.psum_tensor("pu1", [128, 512], F32) as pu1,
        nc.psum_tensor("pshA", [128, 512], F32) as pshA,
        nc.psum_tensor("pshB", [128, 512], F32) as pshB,
        nc.psum_tensor("pxc", [128, 1024], F32) as pxc,
    ):
        psg = [pg0, pg1]
        psu = [pu0, pu1]
        gd, silu_ms, stt_ms, stt_g, stt_u1 = {}, {}, {}, {}, {}
        lastsh = None
        T.wait_ge(dROW.h, sx_row_cnt)
        T.matmul(pxc[:, 0:512], cm128[:], sx2[:, 0:512], start=True, stop=True)
        t_xc = tT.inc(T.matmul(pxc[:, 512:1024], cm128[:], sx2[:, 512:1024],
                               start=True, stop=True))
        S.wait_ge(tT.h, t_xc)
        s_xc = sdo(lambda: S.copy(sxTb[:], pxc[:]))
        V.wait_ge(tS.h, s_xc)
        if dbg:
            V.wait_ge(dDBG.h, dDBG.n)
            G.wait_ge(dDBG.h, dDBG.n)
        V.wait_ge(dAS2[0].h, dAS2[0].n)
        V.wait_ge(dAS2[1].h, dAS2[1].n)
        G.wait_ge(tT.h, tT.n)
        for ic in range(ICN):
            par = ic % 2
            if ic >= 2:
                G.wait_ge(tT.h, gd[ic - 2])
            G.dma_start(gt[par][:, :, :], g_pre[ic]).then_inc(dG2[par].h, 16)
            G.dma_start(ut[par][:, :, :], u_pre[ic]).then_inc(dU2[par].h, 16)
            dG2[par].n += 16
            dU2[par].n += 16
            T.wait_ge(dG2[par].h, 16 * (ic // 2 + 1))
            T.wait_ge(dU2[par].h, 16 * (ic // 2 + 1))
            ms = {}
            for nm, ps_pair, wt in (("g", psg, gt[par]), ("u", psu, ut[par])):
                if ic >= 1:
                    ref = stt_g if nm == "g" else stt_u1
                    T.wait_ge(tV.h, ref[(ic - 1, 0)])
                    T.wait_ge(tV.h, ref[(ic - 1, 1)])
                for ci in range(IFC):
                    for sl in range(2):
                        mm = T.matmul(ps_pair[sl][:, :], wt[:, ci, :],
                                      xn[:, ci, 512 + 512 * sl:1024 + 512 * sl],
                                      start=(ci == 0), stop=(ci == IFC - 1))
                        if ci == IFC - 1:
                            ms[(nm, sl)] = tT.inc(mm)
            gd[ic] = ms[("u", 1)]
            for sl in range(2):
                sgs = sg[par][:, 512 * sl:512 * sl + 512]
                V.wait_ge(tT.h, ms[("g", sl)])
                stt_g[(ic, sl)] = vdo(lambda sgs=sgs, sl=sl: V.scalar_tensor_tensor(
                    sgs, psg[sl][:, :], 1.0, sxTb[:, 512 * sl:512 * sl + 512],
                    ALU.mult, ALU.add))
                S.wait_ge(tV.h, stt_g[(ic, sl)])
                silu_ms[(ic, sl)] = sdo(lambda ic=ic, sgs=sgs: S.activation(
                    sgs, sgs, AF.Silu, scale=gs_s[:, ic:ic + 1]))
            for sl in range(2):
                hts = h_t[:, ic, 512 * sl:512 * sl + 512]
                V.wait_ge(tT.h, ms[("u", sl)])
                stt_u1[(ic, sl)] = vdo(lambda hts=hts, sl=sl: V.scalar_tensor_tensor(
                    hts, psu[sl][:, :], 1.0, sxTb[:, 512 * sl:512 * sl + 512],
                    ALU.mult, ALU.add))
            for sl in range(2):
                sgs = sg[par][:, 512 * sl:512 * sl + 512]
                hts = h_t[:, ic, 512 * sl:512 * sl + 512]
                V.wait_ge(tS.h, silu_ms[(ic, sl)])
                stt_ms[(ic, sl)] = vdo(lambda ic=ic, hts=hts, sgs=sgs: V.scalar_tensor_tensor(
                    hts, hts, us_s[:, ic:ic + 1], sgs, ALU.mult, ALU.mult))
            for icd in ([ic - 1] if ic >= 1 else []) + ([ic] if ic == ICN - 1 else []):
                for sl in range(2):
                    T.wait_ge(tV.h, stt_ms[(icd, sl)])
                    lastsh = T.matmul((pshA if sl == 0 else pshB)[0:1, :], ones128[:],
                                      h_t[:, icd, 512 * sl:512 * sl + 512],
                                      start=(icd == 0), stop=(icd == ICN - 1))
        t_sh = tT.inc(lastsh)
        V.wait_ge(tT.h, t_sh)
        vdo(lambda: V.tensor_copy(sh2[0:1, 0:512], pshA[0:1, :]))
        vdo(lambda: V.tensor_copy(sh2[0:1, 512:1024], pshB[0:1, :]))
        vdo(lambda: V.tensor_sub(shlo[:, 0:512], pshA[0:1, :], sh2[0:1, 0:512]))
        v_shl = vdo(lambda: V.tensor_sub(shlo[:, 512:1024], pshB[0:1, :], sh2[0:1, 512:1024]))
        G.wait_ge(tV.h, v_shl)
        G.wait_ge(dROW.h, dROW.n)
        G.dma_start(sh2[1:2, :], shlo[:]).then_inc(dROW.h, 16)
        dROW.n += 16
        sh_row_cnt = dROW.n

    barrier()
    if dbg:
        SY.dma_start(dbg_outs["dbg_h"],
                     h_t[:, 0:2, :].rearrange("p a b -> p (a b)")).then_inc(dDBG.h, 16)
        SY.wait_ge(dROW.h, sh_row_cnt)
        SY.dma_start(dbg_outs["dbg_sh"], sh2[:]).then_inc(dDBG.h, 16)
        dDBG.n += 32

    with (
        nc.psum_tensor("pe0", [128, 512], F32) as pe0,
        nc.psum_tensor("pe1", [128, 512], F32) as pe1,
        nc.psum_tensor("pe2", [128, 512], F32) as pe2,
        nc.psum_tensor("pe3", [128, 512], F32) as pe3,
        nc.psum_tensor("pe4", [128, 512], F32) as pe4,
        nc.psum_tensor("pe5", [128, 512], F32) as pe5,
        nc.psum_tensor("pe6", [128, 512], F32) as pe6,
        nc.psum_tensor("pe7", [128, 512], F32) as pe7,
    ):
        psD = [[pe0, pe1, pe2, pe3], [pe4, pe5, pe6, pe7]]
        x_r16 = x_f.rearrange("(c p) s -> c p s", p=128)
        y_r = y.rearrange("(c p) t -> c p t", p=128)
        dgrp_ms = {}
        ev_ms = {0: 0, 1: 0}
        kq = 0
        G.wait_ge(tT.h, tT.n)
        for tl in range(2):
            for q4 in range(4):
                pp = kq % 2
                SY.wait_ge(tV.h, ev_ms[pp])
                SY.dma_start(
                    xres[pp][:, :, :],
                    x_r16[4 * q4:4 * q4 + 4, :, 512 + 512 * tl:1024 + 512 * tl].transpose([1, 0, 2]),
                ).then_inc(dXR2[pp].h, 16)
                dXR2[pp].n += 16
                if kq == 0:
                    SY.wait_ge(dAS2[0].h, dAS2[0].n)
                    SY.wait_ge(dAS2[1].h, dAS2[1].n)
                SY.dma_start(
                    arel[pp][:, :, :],
                    attn_scr[4 * q4:4 * q4 + 4, :, 512 * tl:512 * tl + 512].transpose([1, 0, 2]),
                ).then_inc(dAR2[pp].h, 16)
                dAR2[pp].n += 16
                T.wait_ge(tV.h, ev_ms[pp])
                T.wait_ge(dROW.h, sh_row_cnt)
                for ig in range(8):
                    kk = kq * 8 + ig
                    dpar = kk % 3
                    if kk >= 3:
                        G.wait_ge(tT.h, dgrp_ms[kk - 3])
                    if kk < 2:
                        G.wait_ge(dWC.h, dWC.n)
                    G.dma_start(
                        dtl[dpar][:, :, :, :],
                        wd_scr[8 * ig:8 * ig + 8, :, 4 * q4:4 * q4 + 4, :].transpose([1, 0, 2, 3]),
                    ).then_inc(dD2[dpar].h, 16)
                    dD2[dpar].n += 16
                    T.wait_ge(dD2[dpar].h, 16 * (kk // 3 + 1))
                    last = None
                    for ii in range(8):
                        ic = 8 * ig + ii
                        for dmci in range(4):
                            last = T.matmul(psD[pp][dmci][:, :],
                                            dtl[dpar][:, ii, dmci, :],
                                            h_t[:, ic, 512 * tl:512 * tl + 512],
                                            start=(ic == 0), stop=False,
                                            skip_group_check=True)
                    dgrp_ms[kk] = tT.inc(last)
                last = None
                for dmci in range(4):
                    last = T.matmul(psD[pp][dmci][:, :], cm128[:],
                                    sh2[:, 512 * tl:512 * tl + 512],
                                    start=False, stop=True, skip_group_check=True)
                m_d = tT.inc(last)
                V.wait_ge(tT.h, m_d)
                V.wait_ge(dXR2[pp].h, dXR2[pp].n)
                V.wait_ge(dAR2[pp].h, dAR2[pp].n)
                if kq >= 2:
                    V.wait_ge(dOUT2[pp].h, 16 * (kq // 2))
                ev = None
                for dmci in range(4):
                    dmc = 4 * q4 + dmci
                    vdo(lambda pp=pp, dmci=dmci, dmc=dmc: V.scalar_tensor_tensor(
                        outst[pp][:, dmci, :], psD[pp][dmci][:, :],
                        ds_s[:, dmc:dmc + 1], xres[pp][:, dmci, :],
                        ALU.mult, ALU.add))
                    ev = vdo(lambda pp=pp, dmci=dmci: V.tensor_add(
                        outst[pp][:, dmci, :], outst[pp][:, dmci, :], arel[pp][:, dmci, :]))
                ev_ms[pp] = ev
                SY.wait_ge(tV.h, ev)
                SY.dma_start(
                    y_r[4 * q4:4 * q4 + 4, :, 512 * tl:512 * tl + 512].transpose([1, 0, 2]),
                    outst[pp][:, :, :]).then_inc(dOUT2[pp].h, 16)
                dOUT2[pp].n += 16
                kq += 1
        SY.wait_ge(dOUT2[0].h, dOUT2[0].n)
        SY.wait_ge(dOUT2[1].h, dOUT2[1].n)
        if dbg:
            SY.wait_ge(dDBG.h, dDBG.n)

    return nc, stack


def _host_inputs(L0_attn_x, L0_attn_w_qkv, L0_attn_w_out, L0_attn_q_norm_w,
                 L0_attn_k_norm_w, L0_ffn_gate_w, L0_ffn_gate_scale,
                 L0_ffn_up_w, L0_ffn_up_scale, L0_ffn_down_w, L0_ffn_down_scale):
    f32 = np.float32
    x = np.asarray(L0_attn_x, f32)
    wqkv = np.ascontiguousarray(np.asarray(L0_attn_w_qkv, f32))
    wout = np.ascontiguousarray(np.asarray(L0_attn_w_out, f32))
    qnw = np.ascontiguousarray(np.asarray(L0_attn_q_norm_w, f32).reshape(1, HD))
    knw = np.ascontiguousarray(np.asarray(L0_attn_k_norm_w, f32).reshape(1, HD))
    gw = np.asarray(L0_ffn_gate_w, np.int32)
    uw = np.asarray(L0_ffn_up_w, np.int32)
    dw = np.asarray(L0_ffn_down_w, np.int32)

    wq_pre = np.ascontiguousarray(wqkv.T.reshape(IFC, 128, 3072))
    wo_pre = np.ascontiguousarray(
        wout.T.reshape(IFC, 128, 4, 4, 128).transpose(2, 0, 1, 3, 4))
    g_pre = np.ascontiguousarray(gw.reshape(ICN, 128, IFC, 128).transpose(0, 3, 2, 1))
    u_pre = np.ascontiguousarray(uw.reshape(ICN, 128, IFC, 128).transpose(0, 3, 2, 1))
    d_pre = np.ascontiguousarray(dw.reshape(IFC, 128, ICN, 128).transpose(2, 3, 0, 1))
    gs_pre = np.ascontiguousarray(np.asarray(L0_ffn_gate_scale, f32).reshape(ICN, 128).T)
    us_pre = np.ascontiguousarray(np.asarray(L0_ffn_up_scale, f32).reshape(ICN, 128).T)
    ds_pre = np.ascontiguousarray(np.asarray(L0_ffn_down_scale, f32).reshape(IFC, 128).T)

    in_maps = []
    for core in range(NCORES):
        b, half = core // 2, core % 2
        off = 1024 * half
        xc = np.zeros((SLOTS, DM), f32)
        lo = max(0, off - 512)
        s0 = lo - (off - 512)
        xc[s0:, :] = x[b, lo:off + 1024, :]
        x_fm = np.ascontiguousarray(xc.T)
        pos = (np.arange(SLOTS, dtype=f32) + (off - 512))
        invf = (10000.0 ** (-(np.arange(0, HD, 2, dtype=f32) / HD)))
        ang = pos[:, None] * invf[None, :]
        cos = np.concatenate([np.cos(ang), np.cos(ang)], axis=1)
        sin = np.concatenate([-np.sin(ang), np.sin(ang)], axis=1)
        cos_pre = np.ascontiguousarray(cos.reshape(TC, 128, HD).transpose(1, 0, 2))
        sin_pre = np.ascontiguousarray(sin.reshape(TC, 128, HD).transpose(1, 0, 2))
        expb_pre = np.full((128, 4), NEG if half == 0 else 0.0, f32)
        in_maps.append({
            "x_f": x_fm, "wq_pre": wq_pre, "wo_pre": wo_pre,
            "g_pre": g_pre, "u_pre": u_pre, "d_pre": d_pre,
            "gs_r": gs_pre, "us_r": us_pre, "ds_r": ds_pre,
            "cos_in": cos_pre, "sin_in": sin_pre,
            "qnw_in": qnw, "knw_in": knw, "expb_in": expb_pre,
        })
    return in_maps


def kernel(**inputs):
    in_maps = _host_inputs(**inputs)
    nc, stack = build_program()
    res = run_bass_kernel_spmd(nc, in_maps, core_ids=list(range(NCORES)))
    out = np.empty((BATCH, SEQ, DM), np.float32)
    for core in range(NCORES):
        b, half = core // 2, core % 2
        off = 1024 * half
        out[b, off:off + 1024, :] = res.results[core]["y"].T
    return out
